# version 24
# baseline (speedup 1.0000x reference)
import numpy as np
import ml_dtypes
from contextlib import ExitStack

import concourse.bass as bass
import concourse.mybir as mybir
from concourse.bass_utils import run_bass_kernel_spmd

F32 = mybir.dt.float32
BF16 = mybir.dt.bfloat16
AF = mybir.ActivationFunctionType
ALU = mybir.AluOpType
AX = mybir.AxisListType

S, D, H, DH, DFF = 4096, 2048, 16, 128, 8192
CC = D // 128
TC = S // 128
FC = DFF // 128
SB = 512
NSB = S // SB
NFS = 16
NCB = 4
INV_SQRT_DH = float(1.0 / np.sqrt(128.0))
LN_EPS = 1e-5
TWO_PI = float(2.0 * np.pi)
RND_MAGIC = 12582912.0
W_COUNT = float(DFF * D)


class Ctr:
    def __init__(self, stack, nc, name):
        self.h = stack.enter_context(nc.semaphore(name))
        self.n = 0

    def inc(self, k=1):
        self.n += k
        return self.n


class SlotCtr:
    def __init__(self, stack, nc, name, nslots):
        self.h = [stack.enter_context(nc.semaphore(f"{name}{i}")) for i in range(nslots)]
        self.n = [0] * nslots

    def inc(self, slot, k=16):
        self.n[slot] += k
        return self.n[slot]


def build(debug=False, last_phase=3, s3_mode='full'):
    nc = bass.Bass("TRN2", target_bir_lowering=False)

    x_in = nc.dram_tensor("x", [S, D], F32, kind="ExternalInput").ap()
    wqT = nc.dram_tensor("wqT", [D, D], F32, kind="ExternalInput").ap()
    wkT = nc.dram_tensor("wkT", [D, D], F32, kind="ExternalInput").ap()
    wvT = nc.dram_tensor("wvT", [D, D], F32, kind="ExternalInput").ap()
    woT = nc.dram_tensor("woT", [D, D], F32, kind="ExternalInput").ap()
    w1s = nc.dram_tensor("w1s", [NFS, CC, 128, 512], F32, kind="ExternalInput").ap()
    w2s = nc.dram_tensor("w2s", [NCB, FC, 128, 512], F32, kind="ExternalInput").ap()
    bqT_in = nc.dram_tensor("bqT", [128, H], F32, kind="ExternalInput").ap()
    bkT_in = nc.dram_tensor("bkT", [128, H], F32, kind="ExternalInput").ap()
    rotT_in = nc.dram_tensor("rotT", [128, H], F32, kind="ExternalInput").ap()
    spT_in = nc.dram_tensor("spT", [128, H], F32, kind="ExternalInput").ap()
    b1T_in = nc.dram_tensor("b1T", [128, FC], F32, kind="ExternalInput").ap()
    bo_rep = nc.dram_tensor("bo_rep", [128, D], F32, kind="ExternalInput").ap()
    b2_rep = nc.dram_tensor("b2_rep", [128, D], F32, kind="ExternalInput").ap()
    bvT_in = nc.dram_tensor("bvT", [128, H], F32, kind="ExternalInput").ap()

    out_hbm = nc.dram_tensor("out", [S, D], F32, kind="ExternalOutput").ap()

    xn_sc = nc.dram_tensor("xn_sc", [S, D], BF16, kind="Internal").ap()
    u_sc = nc.dram_tensor("u_sc", [H, NSB, 128, SB], BF16, kind="Internal").ap()
    w1q = nc.dram_tensor("w1q", [NFS, CC, 128, 512], BF16, kind="Internal").ap()
    w2q = nc.dram_tensor("w2q", [NCB, FC, 128, 512], BF16, kind="Internal").ap()
    woq = nc.dram_tensor("woq", [CC, 128, D], BF16, kind="Internal").ap()
    out_scr = nc.dram_tensor("out_scr", [S, D], F32, kind="Internal").ap()
    xnT_hbm = nc.dram_tensor("xnT_hbm", [CC, 128, S], BF16, kind="Internal").ap()

    if debug:
        dbg_ht = nc.dram_tensor("dbg_ht", [128, FC * SB], BF16, kind="ExternalOutput").ap()
        dbg_xnt = nc.dram_tensor("dbg_xnt", [128, CC * SB], BF16, kind="ExternalOutput").ap()
        dbg_ub3 = nc.dram_tensor("dbg_ub3", [128, H * SB], BF16, kind="ExternalOutput").ap()
        dbg_xn = nc.dram_tensor("dbg_xn", [S, D], BF16, kind="ExternalOutput").ap()
        dbg_u = nc.dram_tensor("dbg_u", [H, NSB, 128, SB], BF16, kind="ExternalOutput").ap()
        dbg_w1q = nc.dram_tensor("dbg_w1q", [NFS, CC, 128, 512], BF16, kind="ExternalOutput").ap()
        dbg_w2q = nc.dram_tensor("dbg_w2q", [NCB, FC, 128, 512], BF16, kind="ExternalOutput").ap()

    ones_bf = nc.inline_tensor(np.ones((128, 128), dtype=ml_dtypes.bfloat16), "ones_bf").ap()
    ones_r = nc.inline_tensor(np.ones((1, 128), dtype=np.float32), "ones_r").ap()
    ones_c = nc.inline_tensor(np.ones((128, 1), dtype=np.float32), "ones_c").ap()
    zeros_b = nc.inline_tensor(np.zeros((128, 128), dtype=ml_dtypes.bfloat16), "zeros_b").ap()
    ident_b = nc.inline_tensor(np.eye(128, dtype=ml_dtypes.bfloat16), "ident_b").ap()

    with ExitStack() as root:
        e = root.enter_context

        p_bqT = e(nc.sbuf_tensor("p_bqT", [128, H], F32))
        p_bkT = e(nc.sbuf_tensor("p_bkT", [128, H], F32))
        p_bvT = e(nc.sbuf_tensor("p_bvT", [128, H], F32))
        p_qsc = e(nc.sbuf_tensor("p_qsc", [128, H], F32))
        p_ksc = e(nc.sbuf_tensor("p_ksc", [128, H], F32))
        p_bqe = e(nc.sbuf_tensor("p_bqe", [128, H], F32))
        p_bke = e(nc.sbuf_tensor("p_bke", [128, H], F32))
        p_b1T = e(nc.sbuf_tensor("p_b1T", [128, FC], F32))
        p_ones = e(nc.sbuf_tensor("p_ones", [128, 128], BF16))
        p_onesr = e(nc.sbuf_tensor("p_onesr", [1, 128], F32))
        p_onesc = e(nc.sbuf_tensor("p_onesc", [128, 1], F32))
        p_zeros = e(nc.sbuf_tensor("p_zeros", [128, 128], BF16))
        p_identb = e(nc.sbuf_tensor("p_identb", [128, 128], BF16))
        p_thr1 = e(nc.sbuf_tensor("p_thr1", [128, 1], F32))
        p_thr2 = e(nc.sbuf_tensor("p_thr2", [128, 1], F32))
        p_nthr1 = e(nc.sbuf_tensor("p_nthr1", [128, 1], F32))
        p_nthr2 = e(nc.sbuf_tensor("p_nthr2", [128, 1], F32))
        p_t1 = e(nc.sbuf_tensor("p_t1", [128, H], F32))
        p_t2 = e(nc.sbuf_tensor("p_t2", [128, H], F32))
        p_t3 = e(nc.sbuf_tensor("p_t3", [128, H], F32))
        p_thrs = e(nc.sbuf_tensor("p_thrs", [1, 2], F32))
        p_eps = e(nc.sbuf_tensor("p_eps", [128, 1], F32))
        p_spc = e(nc.sbuf_tensor("p_spc", [128, 512], F32))

        def spacer(v):
            v.tensor_copy(p_spc[:, 0:512], p_spc[:, 0:512])

        pbA = e(nc.psum_tensor("pbA", [128, 1024], F32))
        pb_mid = [e(nc.psum_tensor(f"pbm{i}", [128, 512], F32)) for i in range(4)]
        pbB = e(nc.psum_tensor("pbB", [128, 1024], F32))

        def BK(i):
            if i == 0:
                return pbA[:, 0:512]
            if i == 1:
                return pbA[:, 512:1024]
            if i == 6:
                return pbB[:, 0:512]
            if i == 7:
                return pbB[:, 512:1024]
            return pb_mid[i - 2][:]

        s_p = Ctr(root, nc, "s_p")
        s_pc = Ctr(root, nc, "s_pc")
        s_pe0 = Ctr(root, nc, "s_pe0")
        s_ld0 = SlotCtr(root, nc, "s_ld0", 2)
        s_dq = Ctr(root, nc, "s_dq")
        s_sg = Ctr(root, nc, "s_sg")
        s_cw = SlotCtr(root, nc, "s_cw", 2)
        s_woc = Ctr(root, nc, "s_woc")
        s_x = SlotCtr(root, nc, "s_x", 4)
        s_ln = Ctr(root, nc, "s_ln")
        s_lna = Ctr(root, nc, "s_lna")
        s_xnw = SlotCtr(root, nc, "s_xnw", 2)
        s_b1 = Ctr(root, nc, "s_b1")
        s_thr = Ctr(root, nc, "s_thr")
        s_xt = Ctr(root, nc, "s_xt")
        s_dad = Ctr(root, nc, "s_dad")
        s_dac = Ctr(root, nc, "s_dac")
        s_dnf = Ctr(root, nc, "s_dnf")
        s_tp = Ctr(root, nc, "s_tp")
        s_xnT = Ctr(root, nc, "s_xnT")
        s_wv = Ctr(root, nc, "s_wv")
        s_wqk = Ctr(root, nc, "s_wqk")
        s_pvm = Ctr(root, nc, "s_pvm")
        s_vev = Ctr(root, nc, "s_vev")
        s_qkm = Ctr(root, nc, "s_qkm")
        s_qke = Ctr(root, nc, "s_qke")
        s_stm = Ctr(root, nc, "s_stm")
        s_exp = Ctr(root, nc, "s_exp")
        s_avd = Ctr(root, nc, "s_avd")
        s_nrm = Ctr(root, nc, "s_nrm")
        s_uw = SlotCtr(root, nc, "s_uw", 2)
        s_b2 = Ctr(root, nc, "s_b2")
        s_bb = Ctr(root, nc, "s_bb")
        s_ul = Ctr(root, nc, "s_ul")
        s_xh = Ctr(root, nc, "s_xh")
        s_xnl = SlotCtr(root, nc, "s_xnl", 2)
        s_w1l = SlotCtr(root, nc, "s_w1l", 4)
        s_w1d = Ctr(root, nc, "s_w1d")
        s_w2l = SlotCtr(root, nc, "s_w2l", 3)
        s_w2d = Ctr(root, nc, "s_w2d")
        s_wol = SlotCtr(root, nc, "s_wol", 4)
        s_wod = Ctr(root, nc, "s_wod")
        s_xbl = SlotCtr(root, nc, "s_xbl", 4)
        s_hm = Ctr(root, nc, "s_hm")
        s_gel = Ctr(root, nc, "s_gel")
        s_om = Ctr(root, nc, "s_om")
        s_oev = Ctr(root, nc, "s_oev")
        s_ow = SlotCtr(root, nc, "s_ow", 2)
        s_owd = Ctr(root, nc, "s_owd")

        block = e(nc.Block())

        @block.sync
        def _(sync):
            for dst, src in [(p_bqT, bqT_in), (p_bkT, bkT_in), (p_bvT, bvT_in),
                             (p_t1, rotT_in),
                             (p_t2, spT_in), (p_b1T, b1T_in)]:
                sync.dma_start(out=dst[:], in_=src).then_inc(s_p.h, 16)
                s_p.inc(16)
            for dst, src in [(p_ones, ones_bf), (p_onesr, ones_r), (p_onesc, ones_c),
                             (p_zeros, zeros_b), (p_identb, ident_b)]:
                sync.dma_start(out=dst[:], in_=src).then_inc(s_p.h, 16)
                s_p.inc(16)
        p_loads = s_p.n

        @block.vector
        def _(v):
            v.wait_ge(s_p.h, p_loads)
            v.memset(p_eps[:], LN_EPS)
            v.tensor_scalar(p_t3[:], p_t1[:], 1.0 / TWO_PI, RND_MAGIC, ALU.mult, ALU.add)
            spacer(v)
            v.tensor_scalar(p_t3[:], p_t3[:], RND_MAGIC, None, ALU.subtract)
            spacer(v)
            v.scalar_tensor_tensor(p_ksc[:], p_t3[:], -TWO_PI, p_t1[:],
                                   ALU.mult, ALU.add)
            spacer(v)
            v.tensor_copy(p_spc[:, 0:16], p_ksc[:]).then_inc(s_pc.h, 1)
            s_pc.inc()
        rr_done = s_pc.n

        @block.scalar
        def _(s):
            s.wait_ge(s_pc.h, rr_done)
            s.activation(p_t3[:], p_ksc[:], AF.Sin, scale=0.5).then_inc(s_sg.h, 1)
            s_sg.inc()
            s.activation(p_t1[:], p_ksc[:], AF.Sin).then_inc(s_sg.h, 1)
            s_sg.inc()
        trig_acts = s_sg.n

        @block.vector
        def _(v):
            v.wait_ge(s_sg.h, trig_acts)
            v.tensor_tensor(p_qsc[:], p_t3[:], p_t3[:], ALU.mult)
            spacer(v)
            v.tensor_scalar(p_qsc[:], p_qsc[:], -2.0, 1.0, ALU.mult, ALU.add)
            spacer(v)
            v.tensor_tensor(p_qsc[:], p_qsc[:], p_t2[:], ALU.mult)
            spacer(v)
            v.tensor_scalar(p_qsc[:], p_qsc[:], INV_SQRT_DH, None, ALU.mult)
            spacer(v)
            v.tensor_copy(p_ksc[:], p_t1[:])
            spacer(v)
            v.tensor_tensor(p_bqe[:], p_bqT[:], p_qsc[:], ALU.mult)
            spacer(v)
            v.tensor_tensor(p_bke[:], p_bkT[:], p_ksc[:], ALU.mult)
            spacer(v)
            v.tensor_copy(p_spc[:, 0:16], p_qsc[:]).then_inc(s_pc.h, 1)
            s_pc.inc()
        trig_done = s_pc.n

        s12 = ExitStack()
        xnT = s12.enter_context(nc.sbuf_tensor("xnT", [128, CC * S], BF16))

        with ExitStack() as s01:
            e01 = s01.enter_context
            xti = [e01(nc.sbuf_tensor(f"xti{i}", [128, D], F32)) for i in range(4)]
            xnb = [e01(nc.sbuf_tensor(f"xnb{i}", [128, D], BF16)) for i in range(2)]
            lnscr = [e01(nc.sbuf_tensor(f"lnscr{i}", [128, D], BF16)) for i in range(2)]
            st_sums = e01(nc.sbuf_tensor("st_sums", [128, TC], F32))
            st_sumsq = e01(nc.sbuf_tensor("st_sumsq", [128, TC], F32))
            st_nmean = e01(nc.sbuf_tensor("st_nmean", [128, TC], F32))
            st_nm2 = e01(nc.sbuf_tensor("st_nm2", [128, TC], F32))
            st_msq = e01(nc.sbuf_tensor("st_msq", [128, TC], F32))
            st_var = e01(nc.sbuf_tensor("st_var", [128, TC], F32))
            st_sqv = e01(nc.sbuf_tensor("st_sqv", [128, TC], F32))
            st_rstd = e01(nc.sbuf_tensor("st_rstd", [128, TC], F32))

            def x_loads(eng, par):
                for i in range(par, TC, 2):
                    if i >= 4:
                        eng.wait_ge(s_ln.h, 2 * (i - 3))
                        eng.wait_ge(s_lna.h, i - 3)
                    eng.dma_start(out=xti[i % 4][:],
                                  in_=x_in[i * 128:(i + 1) * 128, :]).then_inc(s_x.h[i % 4], 16)
                    s_x.inc(i % 4)
                for i in range(par, TC, 2):
                    if i < 4:
                        eng.wait_ge(s_ln.h, 2 * (TC - 4 + i) + 2)
                        eng.wait_ge(s_lna.h, TC - 4 + i + 1)
                    else:
                        eng.wait_ge(s_ln.h, 66 + i - 3)
                    eng.dma_start(out=xti[i % 4][:],
                                  in_=x_in[i * 128:(i + 1) * 128, :]).then_inc(s_x.h[i % 4], 16)
                    s_x.inc(i % 4)

            @block.sync
            def _(sync):
                x_loads(sync, 0)

            @block.gpsimd
            def _(g):
                x_loads(g, 1)

            @block.scalar
            def _(s):
                for i in range(TC):
                    s.wait_ge(s_x.h[i % 4], 16 * (i // 4 + 1))
                    if i >= 2:
                        s.wait_ge(s_ln.h, 2 * (i - 1))
                    s.activation(lnscr[i % 2][:], xti[i % 4][:],
                                 AF.Square).then_inc(s_lna.h, 1)
                    s_lna.inc()
                s.wait_ge(s_ln.h, 65)
                s.activation(st_sqv[:], st_var[:], AF.Sqrt,
                             bias=p_eps[:, 0:1]).then_inc(s_lna.h, 1)
                s_lna.inc()
            sqv_done = s_lna.n

            @block.vector
            def _(v):
                for i in range(TC):
                    v.wait_ge(s_x.h[i % 4], 16 * (i // 4 + 1))
                    v.reduce_sum(st_sums[:, i:i + 1], xti[i % 4][:],
                                 AX.X).then_inc(s_ln.h, 1)
                    v.wait_ge(s_lna.h, i + 1)
                    v.reduce_sum(st_sumsq[:, i:i + 1], lnscr[i % 2][:],
                                 AX.X).then_inc(s_ln.h, 1)
                spacer(v)
                v.tensor_scalar(st_nmean[:], st_sums[:], -1.0 / D, None, ALU.mult)
                spacer(v)
                v.tensor_tensor(st_nm2[:], st_nmean[:], st_nmean[:], ALU.mult)
                v.tensor_scalar(st_msq[:], st_sumsq[:], 1.0 / D, None, ALU.mult)
                spacer(v)
                v.tensor_tensor(st_var[:], st_msq[:], st_nm2[:], ALU.subtract)
                spacer(v)
                v.tensor_copy(p_spc[:, 0:32], st_var[:]).then_inc(s_ln.h, 1)
                s_ln.inc(2 * TC + 1)
                v.wait_ge(s_lna.h, sqv_done)
                v.reciprocal(st_rstd[:], st_sqv[:])
                spacer(v)
                v.tensor_copy(p_spc[:, 0:32], st_rstd[:]).then_inc(s_ln.h, 1)
                s_ln.inc(1)
                for i in range(TC):
                    v.wait_ge(s_x.h[i % 4], 16 * (TC // 4 + i // 4 + 1))
                    if i >= 2:
                        v.wait_ge(s_tp.h, (i - 1) * CC)
                    v.tensor_scalar(xnb[i % 2][:], xti[i % 4][:],
                                    st_nmean[:, i:i + 1], st_rstd[:, i:i + 1],
                                    ALU.add, ALU.mult).then_inc(s_ln.h, 1)
                    s_ln.inc(1)

            @block.tensor
            def _(t):
                t.wait_ge(s_p.h, p_loads)
                pAb = pbA.bitcast(BF16)
                for i in range(TC):
                    t.wait_ge(s_ln.h, 66 + i + 1)
                    for c in range(CC):
                        j = i * CC + c
                        if j >= 2:
                            t.wait_ge(s_xt.h, j - 1)
                        t.transpose(pAb[:, (j % 2) * 1024:(j % 2) * 1024 + 128],
                                    xnb[i % 2][:, c * 128:(c + 1) * 128],
                                    p_identb[:]).then_inc(s_tp.h, 1)
                        s_tp.inc()

            @block.scalar
            def _(s):
                for i in range(TC):
                    for c in range(CC):
                        j = i * CC + c
                        s.wait_ge(s_tp.h, j + 1)
                        s.copy(xnT[:, c * S + i * 128: c * S + (i + 1) * 128],
                               pbA.bitcast(BF16)[:, (j % 2) * 1024:(j % 2) * 1024 + 128]
                               ).then_inc(s_xt.h, 1)
                        s_xt.inc()
            xt_done = s_xt.n

        @block.sync
        def _(sync):
            sync.wait_ge(s_xt.h, xt_done)
            sync.nop().then_inc(s_b1.h, 1)
            s_b1.inc()
        b1v = s_b1.n

        def hs_of(h, sb):
            return h * NSB + sb

        def cnt(hs, tc):
            return hs * TC + tc + 1

        VB = (0, 1, 6, 7)

        def pcnt(hs):
            return (hs + 1) * (TC // 2)

        if last_phase < 2:
            @block.gpsimd
            def _(g):
                g.wait_ge(s_xt.h, xt_done)
            return nc

        NPC = 512

        with ExitStack() as s2:
            e2 = s2.enter_context
            vg = e2(nc.sbuf_tensor("vg", [128, TC * 256], BF16))
            wvb = e2(nc.sbuf_tensor("wvb", [128, CC * 256], BF16))
            wqb = e2(nc.sbuf_tensor("wqb", [128, CC * 128], BF16))
            wkb = e2(nc.sbuf_tensor("wkb", [128, CC * 128], BF16))
            qt = e2(nc.sbuf_tensor("qt", [128, S], BF16))
            kt = e2(nc.sbuf_tensor("kt", [128, S], BF16))
            eb = [e2(nc.sbuf_tensor(f"eb{i}", [128, 2 * SB], BF16)) for i in range(2)]
            recb = e2(nc.sbuf_tensor("recb", [128, SB], F32))
            ub = [e2(nc.sbuf_tensor(f"ub{i}", [128, SB], BF16)) for i in range(2)]
            slabp = [e2(nc.sbuf_tensor(f"slabp{i}", [128, 512], F32)) for i in range(2)]
            qbq = [e2(nc.sbuf_tensor(f"qbq{i}", [128, 512], BF16)) for i in range(2)]
            dacc = e2(nc.sbuf_tensor("dacc", [128, 2 * SB], F32))
            dacc_bf = e2(nc.sbuf_tensor("dacc_bf", [128, 2 * SB], BF16))
            redrs = e2(nc.sbuf_tensor("redrs", [128, NPC], F32))
            accr = e2(nc.sbuf_tensor("accr", [128, 2], F32))
            wob = e2(nc.sbuf_tensor("wob", [128, D], BF16))

            w1f = w1s.rearrange("a b p j -> (a b) p j")
            w2f = w2s.rearrange("a b p j -> (a b) p j")
            w1qf = w1q.rearrange("a b p j -> (a b) p j")
            w2qf = w2q.rearrange("a b p j -> (a b) p j")

            def psrc(p):
                return w1f[p] if p < 256 else w2f[p - 256]

            def pdst(p):
                return w1qf[p] if p < 256 else w2qf[p - 256]

            SC_AT = {0: 86, 1: 86, 2: 86, 3: 86, 4: 86, 5: 82}
            QP_AT = {h: 64 for h in range(8, 16)}

            @block.sync
            def _(sync):
                sync.wait_ge(s_b1.h, b1v)
                sync.nop().then_inc(s_xnT.h, 16)
                s_xnT.inc(16)
            xnT_done = s_xnT.n

            @block.sync
            def _(sync):
                sync.wait_ge(s_xnT.h, xnT_done)
                for c in range(CC):
                    sync.dma_start(out=xnT_hbm[c, :, :],
                                   in_=xnT[:, c * S:(c + 1) * S]).then_inc(s_xh.h, 16)
                    s_xh.inc(16)
            xh_done = s_xh.n

            @block.sync
            def _(sync):
                for p in range(NPC):
                    if p >= 2:
                        sync.wait_ge(s_dq.h, p - 1)
                    sync.dma_start(out=slabp[p % 2][:],
                                   in_=psrc(p)).then_inc(s_ld0.h[p % 2], 16)
                    s_ld0.inc(p % 2)
                for p in range(NPC + 2):
                    if 2 <= p < NPC + 2:
                        sync.wait_ge(s_dq.h, 512 + p)
                    if p < NPC:
                        if p < 2:
                            sync.wait_ge(s_dq.h, 511 + p)
                        sync.dma_start(out=slabp[p % 2][:],
                                       in_=psrc(p)).then_inc(s_ld0.h[p % 2], 16)
                        s_ld0.inc(p % 2)
                    if p >= 2:
                        sync.dma_start(out=pdst(p - 2),
                                       in_=qbq[(p - 2) % 2][:]).then_inc(s_cw.h[(p - 2) % 2], 16)
                        s_cw.inc((p - 2) % 2)

            @block.gpsimd
            def _(g):
                g.wait_ge(s_b1.h, b1v)
                g.nop().then_inc(s_wv.h, 16)
                s_wv.inc(16)

                def u_writes_for_head(hh):
                    for sb in range(NSB):
                        hs = hs_of(hh, sb)
                        g.wait_ge(s_nrm.h, hs + 1)
                        g.dma_start(out=u_sc[hh, sb, :, :],
                                    in_=ub[hs % 2][:]).then_inc(s_uw.h[hs % 2], 16)
                        s_uw.inc(hs % 2)

                def g_inject(h):
                    if 1 <= h <= 4:
                        for dc in range(4 * (h - 1), 4 * h):
                            if dc >= 1:
                                g.wait_ge(s_woc.h, 32 * dc)
                            g.dma_start(out=wob[:],
                                        in_=woT[dc * 128:(dc + 1) * 128, :]).then_inc(s_woc.h, 16)
                            s_woc.inc(16)
                            g.wait_ge(s_woc.h, s_woc.n)
                            g.dma_start(out=woq[dc, :, :], in_=wob[:]).then_inc(s_woc.h, 16)
                            s_woc.inc(16)

                for h in range(H):
                    if h % 2 == 0:
                        g2 = h // 2
                        if g2 >= 1:
                            g.wait_ge(s_pvm.h, g2 * (TC // 2))
                        src = wvT.rearrange("(cc p) (g j) -> p cc g j", p=128, j=256)[:, :, g2, :]
                        g.dma_start(out=wvb.ap().rearrange("p (c j) -> p c j", j=256),
                                    in_=src).then_inc(s_wv.h, 16)
                        s_wv.inc(16)
                    if h >= 1:
                        g.wait_ge(s_qkm.h, h * 16)
                    srcq = wqT.rearrange("(cc p) (h j) -> p cc h j", p=128, j=128)[:, :, h, :]
                    srck = wkT.rearrange("(cc p) (h j) -> p cc h j", p=128, j=128)[:, :, h, :]
                    g.dma_start(out=wqb.ap().rearrange("p (c j) -> p c j", j=128),
                                in_=srcq).then_inc(s_wqk.h, 16)
                    s_wqk.inc(16)
                    g.dma_start(out=wkb.ap().rearrange("p (c j) -> p c j", j=128),
                                in_=srck).then_inc(s_wqk.h, 16)
                    s_wqk.inc(16)
                    if h >= 1:
                        u_writes_for_head(h - 1)
                    g_inject(h)
                u_writes_for_head(H - 1)
            u_writes = list(s_uw.n)
            wo_done = s_woc.n
            cache_done = list(s_cw.n)

            @block.tensor
            def _(t):
                t.wait_ge(s_xnT.h, xnT_done)

                def pe_inject(h):
                    if h == 7:
                        t.wait_ge(s_exp.h, pcnt(hs_of(6, NSB - 1)))
                        t.wait_ge(s_vev.h, 4 * (TC // 2))
                        t.wait_ge(s_dq.h, 513)
                        t.matmul(BK(0)[0:1, 0:2], p_onesc[:, 0:1], accr[:, 0:2],
                                 start=True, stop=True)
                        t.matmul(BK(0)[0:1, 0:128], p_zeros[:, 0:1], p_zeros[:],
                                 start=False, stop=True,
                                 skip_group_check=True).then_inc(s_pe0.h, 1)
                        s_pe0.inc()
                    if h == 8:
                        t.wait_ge(s_exp.h, pcnt(hs_of(7, NSB - 1)))
                        t.wait_ge(s_vev.h, 5 * (TC // 2))
                        t.wait_ge(s_thr.h, 1)
                        t.matmul(BK(0)[0:128, 0:1], p_onesr[0:1, :], p_thrs[0:1, 0:1],
                                 start=True, stop=True)
                        t.matmul(BK(1)[0:128, 0:1], p_onesr[0:1, :], p_thrs[0:1, 1:2],
                                 start=True, stop=True)
                        t.matmul(BK(1)[0:1, 0:128], p_zeros[:, 0:1], p_zeros[:],
                                 start=False, stop=True,
                                 skip_group_check=True).then_inc(s_pe0.h, 1)
                        s_pe0.inc()

                for g2 in range(8):
                    t.wait_ge(s_wv.h, 16 * (g2 + 2))
                    if g2 >= 1:
                        t.wait_ge(s_exp.h, pcnt(hs_of(2 * g2 - 1, NSB - 1)))
                    for tc in range(TC):
                        vp = tc // 2
                        half = tc % 2
                        gp = g2 * (TC // 2) + vp
                        bank = BK(VB[vp % 4])
                        if half == 0 and gp >= 4:
                            t.wait_ge(s_vev.h, gp - 3)
                        for cc in range(CC):
                            mm = t.matmul(bank[:, half * 256:(half + 1) * 256],
                                          xnT[:, cc * S + tc * 128: cc * S + (tc + 1) * 128],
                                          wvb[:, cc * 256:(cc + 1) * 256],
                                          start=(cc == 0), stop=(cc == CC - 1))
                        if half == 1:
                            mm.then_inc(s_pvm.h, 1)
                            s_pvm.inc()
                    for hl in range(2):
                        h = 2 * g2 + hl
                        pe_inject(h)
                        t.wait_ge(s_wqk.h, 32 * (h + 1))
                        if h >= 1:
                            t.wait_ge(s_exp.h, pcnt(hs_of(h - 1, NSB - 1)))
                        if hl == 0:
                            t.wait_ge(s_vev.h, (g2 + 1) * (TC // 2))
                        for qk in range(2):
                            wb = wqb if qk == 0 else wkb
                            for sb in range(NSB):
                                grp = h * 16 + qk * 8 + sb
                                bank = BK(6 + grp % 2)
                                if grp >= 2:
                                    t.wait_ge(s_qke.h, grp - 1)
                                for cc in range(CC):
                                    mm = t.matmul(bank,
                                                  wb[:, cc * 128:(cc + 1) * 128],
                                                  xnT[:, cc * S + sb * SB: cc * S + (sb + 1) * SB],
                                                  start=(cc == 0), stop=(cc == CC - 1))
                                mm.then_inc(s_qkm.h, 1)
                                s_qkm.inc()
                        if hl == 0:
                            t.wait_ge(s_vev.h, (g2 + 1) * (TC // 2))
                        if h == 7:
                            t.wait_ge(s_thr.h, 1)
                        if h == 8:
                            t.wait_ge(s_thr.h, 2)
                        t.wait_ge(s_qke.h, h * 16 + 16)
                        for sb in range(NSB):
                            hs = hs_of(h, sb)
                            ub_bank = BK(2 + hs % 2)
                            den_bank = BK(4 + hs % 2)
                            if hs >= 2:
                                t.wait_ge(s_nrm.h, hs - 1)

                            def st_pair(p, sbq=None):
                                sq = sb if sbq is None else sbq
                                pbank = [pbA, pbB][p % 2]
                                for half in range(2):
                                    tc = 2 * p + half
                                    t.matmul(pbank[:, half * 512:(half + 1) * 512],
                                             kt[:, tc * 128:(tc + 1) * 128],
                                             qt[:, sq * SB:(sq + 1) * SB],
                                             start=True, stop=True).then_inc(s_stm.h, 1)
                                    s_stm.inc()

                            NP = TC // 2
                            if sb == 0:
                                if hs >= 1:
                                    t.wait_ge(s_exp.h, hs * NP - 1)
                                st_pair(0)
                                if hs >= 1:
                                    t.wait_ge(s_exp.h, hs * NP)
                                st_pair(1)
                            for p in range(NP):
                                t.wait_ge(s_exp.h, hs * NP + p + 1)
                                if p >= NP - 2 and sb < NSB - 1:
                                    st_pair(p - (NP - 2), sb + 1)
                                pe_den = p < 4 or p >= 12
                                for half in range(2):
                                    tc = 2 * p + half
                                    mm = t.matmul(ub_bank,
                                                  vg[:, tc * 256 + hl * 128: tc * 256 + (hl + 1) * 128],
                                                  eb[p % 2][:, half * SB:(half + 1) * SB],
                                                  start=(tc == 0), stop=(tc == TC - 1))
                                    if pe_den:
                                        mm = t.matmul(den_bank,
                                                      p_ones[:],
                                                      eb[p % 2][:, half * SB:(half + 1) * SB],
                                                      start=(tc == 0), stop=False)
                                    mm.then_inc(s_avd.h, 1)
                                    s_avd.inc()
                                if p + 2 < NP:
                                    st_pair(p + 2)
                            t.wait_ge(s_dac.h, hs + 1)
                            t.matmul(den_bank, p_ones[:],
                                     dacc_bf[:, 0:SB],
                                     start=False, stop=False)
                            t.matmul(den_bank, p_ones[:],
                                     dacc_bf[:, SB:2 * SB],
                                     start=False, stop=True).then_inc(s_dnf.h, 1)
                            s_dnf.inc()
                t.matmul(BK(0)[:, 0:128], p_zeros[:], p_zeros[:],
                         start=False, stop=True,
                         skip_group_check=True).then_inc(s_b2.h, 1)
                s_b2.inc()

            @block.scalar
            def _(s):
                s.wait_ge(s_pc.h, trig_done)
                for h in range(H):
                    if h % 2 == 0:
                        g2 = h // 2
                        for vp in range(TC // 2):
                            s.wait_ge(s_pvm.h, g2 * (TC // 2) + vp + 1)
                            s.copy(vg[:, vp * 512:(vp + 1) * 512],
                                   BK(VB[vp % 4])).then_inc(s_vev.h, 1)
                            s_vev.inc()
                    for qk in range(2):
                        dstt = qt if qk == 0 else kt
                        bcol = p_bqe if qk == 0 else p_bke
                        scol = p_qsc if qk == 0 else p_ksc
                        for sb in range(NSB):
                            grp = h * 16 + qk * 8 + sb
                            s.wait_ge(s_qkm.h, grp + 1)
                            s.activation(dstt[:, sb * SB:(sb + 1) * SB],
                                         BK(6 + grp % 2), AF.Identity,
                                         bias=bcol[:, h:h + 1],
                                         scale=scol[:, h:h + 1]).then_inc(s_qke.h, 1)
                            s_qke.inc()
                    for sb in range(NSB):
                        hs = hs_of(h, sb)
                        for p in range(TC // 2):
                            s.wait_ge(s_stm.h, cnt(hs, 2 * p + 1))
                            if p >= 2:
                                s.wait_ge(s_avd.h, cnt(hs, 2 * p - 3))
                            elif hs >= 1:
                                s.wait_ge(s_avd.h, cnt(hs - 1, TC - 3 + 2 * p))
                            if 6 <= p <= 13:
                                s.wait_ge(s_dad.h, hs * 8 + p - 5)
                            s.activation(eb[p % 2][:], [pbA, pbB][p % 2][:],
                                         AF.Exp).then_inc(s_exp.h, 1)
                            s_exp.inc()

            @block.vector
            def _(v):
                v.wait_ge(s_xnT.h, xnT_done)
                sc_done = [0]
                qp_done_v = [0]

                def dve_scale_pieces(n):
                    for p in range(sc_done[0], min(NPC, sc_done[0] + n)):
                        v.wait_ge(s_ld0.h[p % 2], 16 * (p // 2 + 1))
                        v.reduce_sum(redrs[:, p:p + 1], slabp[p % 2][:], AX.X,
                                     apply_absolute_value=True).then_inc(s_dq.h, 1)
                        s_dq.inc()
                    sc_done[0] = min(NPC, sc_done[0] + n)

                def dve_quant_pieces(n):
                    for p in range(qp_done_v[0], min(NPC, qp_done_v[0] + n)):
                        thrc = p_thr1 if p < 256 else p_thr2
                        nthrc = p_nthr1 if p < 256 else p_nthr2
                        v.wait_ge(s_ld0.h[p % 2], 16 * (256 + p // 2 + 1))
                        if p >= 2:
                            v.wait_ge(s_cw.h[p % 2], 16 * (p // 2))
                        v.tensor_scalar(recb[:], slabp[p % 2][:], thrc[:, 0:1], -1.0,
                                        ALU.is_ge, ALU.add)
                        v.scalar_tensor_tensor(qbq[p % 2][:], slabp[p % 2][:],
                                               nthrc[:, 0:1], recb[:],
                                               ALU.is_ge, ALU.add).then_inc(s_dq.h, 1)
                        s_dq.inc()
                    qp_done_v[0] = min(NPC, qp_done_v[0] + n)

                def dve_bulk(h, sb):
                    if sb >= NSB - 1:
                        return
                    if h <= 5:
                        dve_scale_pieces(13)
                    elif h >= 8:
                        dve_quant_pieces(10)

                def dve_inject(h):
                    if h == 6:
                        spacer(v)
                        v.reduce_sum(accr[:, 0:1], redrs[:, 0:256], AX.X)
                        spacer(v)
                        v.reduce_sum(accr[:, 1:2], redrs[:, 256:512], AX.X)
                        spacer(v)
                        v.tensor_copy(p_spc[:, 0:2], accr[:, 0:2]).then_inc(s_dq.h, 1)
                        s_dq.inc()
                    if h == 7:
                        v.wait_ge(s_pe0.h, 1)
                        v.tensor_scalar(p_thrs[0:1, 0:2], BK(0)[0:1, 0:2],
                                        0.7 / W_COUNT, None, ALU.mult)
                        spacer(v)
                        v.tensor_copy(p_spc[0:1, 0:2], p_thrs[0:1, 0:2]).then_inc(s_thr.h, 1)
                        s_thr.inc()
                    if h == 8:
                        v.wait_ge(s_pe0.h, 2)
                        v.tensor_copy(p_thr1[:], BK(0)[0:128, 0:1])
                        v.tensor_copy(p_thr2[:], BK(1)[0:128, 0:1])
                        spacer(v)
                        v.tensor_scalar(p_nthr1[:], p_thr1[:], -1.0, None, ALU.mult)
                        v.tensor_scalar(p_nthr2[:], p_thr2[:], -1.0, None, ALU.mult)
                        spacer(v)
                        v.tensor_copy(p_spc[:, 0:1], p_nthr1[:]).then_inc(s_thr.h, 1)
                        s_thr.inc()

                for g2 in range(8):
                    for hl in range(2):
                        h = 2 * g2 + hl
                        dve_inject(h)
                        for sb in range(NSB):
                            hs = hs_of(h, sb)
                            for p in range(4, 12):
                                v.wait_ge(s_exp.h, hs * (TC // 2) + p + 1)
                                if p == 4:
                                    v.tensor_copy(dacc[:], eb[p % 2][:]).then_inc(s_dad.h, 1)
                                else:
                                    v.tensor_tensor(dacc[:], dacc[:], eb[p % 2][:],
                                                    ALU.add).then_inc(s_dad.h, 1)
                                s_dad.inc()
                            v.tensor_copy(dacc_bf[:], dacc[:]).then_inc(s_dac.h, 1)
                            s_dac.inc()
                            dve_bulk(h, sb)
                            v.wait_ge(s_avd.h, cnt(hs, TC - 1))
                            v.wait_ge(s_dnf.h, hs + 1)
                            if hs >= 2:
                                v.wait_ge(s_uw.h[hs % 2], 16 * (hs // 2))
                            v.reciprocal(recb[:], BK(4 + hs % 2))
                            v.tensor_tensor(ub[hs % 2][:], BK(2 + hs % 2),
                                            recb[:], ALU.mult)
                            v.tensor_scalar(ub[hs % 2][:], ub[hs % 2][:],
                                            p_bvT[:, h:h + 1], None,
                                            ALU.add).then_inc(s_nrm.h, 1)
                            s_nrm.inc()
                dve_scale_pieces(NPC)
                dve_quant_pieces(NPC)

            @block.sync
            def _(sync):
                sync.wait_ge(s_uw.h[0], u_writes[0])
                sync.wait_ge(s_uw.h[1], u_writes[1])
                sync.nop().then_inc(s_b2.h, 1)
                s_b2.inc()
        b2v = s_b2.n
        s12.close()

        if last_phase < 3:
            @block.gpsimd
            def _(g):
                g.wait_ge(s_b2.h, b2v)
                if debug:
                    g.dma_start(out=dbg_xn.rearrange("(a p) j -> a p j", p=128),
                                in_=xnT_hbm.rearrange("c p s -> (c p) s").rearrange(
                                    "(a p) j -> a p j", p=128)).then_inc(s_owd.h, 16)
                    s_owd.inc(16)
                    g.dma_start(out=dbg_w1q.rearrange("a b p j -> (a b) p j"),
                                in_=w1q.rearrange("a b p j -> (a b) p j")).then_inc(s_owd.h, 16)
                    s_owd.inc(16)
                    g.dma_start(out=dbg_w2q.rearrange("a b p j -> (a b) p j"),
                                in_=w2q.rearrange("a b p j -> (a b) p j")).then_inc(s_owd.h, 16)
                    s_owd.inc(16)
                    g.dma_start(out=dbg_u.rearrange("h s p j -> (h s) p j"),
                                in_=u_sc.rearrange("h s p j -> (h s) p j")).then_inc(s_owd.h, 16)
                    s_owd.inc(16)
                    g.wait_ge(s_ow.h[0], s_ow.n[0])
                g.wait_ge(s_ow.h[1], s_ow.n[1])
                if s_owd.n:
                    g.wait_ge(s_owd.h, s_owd.n)
            return nc

        with ExitStack() as s3:
            e3 = s3.enter_context
            hT = e3(nc.sbuf_tensor("hT", [128, FC * SB], BF16))
            xnTb = [e3(nc.sbuf_tensor(f"xnTb{i}", [128, CC * SB], BF16)) for i in range(2)]
            w1b = [e3(nc.sbuf_tensor(f"w1b{i}", [128, 8 * SB], BF16)) for i in range(4)]
            w2b = [e3(nc.sbuf_tensor(f"w2b{i}", [128, 8 * SB], BF16)) for i in range(3)]
            wobf = [e3(nc.sbuf_tensor(f"wobf{i}", [128, 4 * SB], BF16)) for i in range(3)]
            ub3 = e3(nc.sbuf_tensor("ub3", [128, H * SB], BF16))
            xbb = [e3(nc.sbuf_tensor(f"xbb{i}", [128, SB], F32)) for i in range(4)]
            outb = [e3(nc.sbuf_tensor(f"outb{i}", [128, SB], F32)) for i in range(2)]
            bob2 = e3(nc.sbuf_tensor("bob2", [128, D], F32))
            bo2s = hT.bitcast(F32)[:, 0:D]

            @block.sync
            def _(sync):
                sync.wait_ge(s_b2.h, b2v)
                sync.dma_start(out=bob2[:], in_=bo_rep).then_inc(s_bb.h, 16)
                s_bb.inc(16)
                sync.dma_start(out=bo2s, in_=b2_rep).then_inc(s_bb.h, 16)
                s_bb.inc(16)
            bb_loads = s_bb.n

            @block.vector
            def _(v):
                v.wait_ge(s_bb.h, bb_loads)
                v.tensor_tensor(bob2[:], bob2[:], bo2s, ALU.add).then_inc(s_bb.h, 1)
                s_bb.inc()
            bb_done = s_bb.n

            @block.sync
            def _(sync):
                def w1_loads(k):
                    if k == 0:
                        for i in range(2):
                            sync.wait_ge(s_cw.h[i], cache_done[i])
                        sync.wait_ge(s_woc.h, wo_done)
                    for fs in range(NFS):
                        for half in range(2):
                            idx = k * 32 + fs * 2 + half
                            if idx >= 4:
                                sync.wait_ge(s_w1d.h, 2 * ((idx - 4) // 2 + 1))
                            sync.dma_start(
                                out=w1b[idx % 4].ap().rearrange("p (c j) -> p c j", j=SB),
                                in_=w1q[fs, half * 8:(half + 1) * 8, :, :].rearrange(
                                    "c p j -> p c j")).then_inc(s_w1l.h[idx % 4], 16)
                            s_w1l.inc(idx % 4)

                def xnt_load(k):
                    sync.dma_start(
                        out=xnTb[k % 2].ap().rearrange("p (c j) -> p c j", j=SB),
                        in_=xnT_hbm[:, :, k * SB:(k + 1) * SB].rearrange(
                            "c p j -> p c j")).then_inc(s_xnl.h[k % 2], 16)
                    s_xnl.inc(k % 2)

                def cb_streams(k):
                    if s3_mode == 'ffn1':
                        return
                    for cb in range(NCB):
                        if s3_mode != 'full':
                            for piece in range(8):
                                idx = (k * NCB + cb) * 8 + piece
                                if idx >= 3:
                                    sync.wait_ge(s_w2d.h, idx - 2)
                                sync.dma_start(
                                    out=w2b[idx % 3].ap().rearrange("p (c j) -> p c j", j=SB),
                                    in_=w2q[cb, piece * 8:(piece + 1) * 8, :, :].rearrange(
                                        "c p j -> p c j")).then_inc(s_w2l.h[idx % 3], 16)
                                s_w2l.inc(idx % 3)
                        if s3_mode == 'full':
                            for piece in range(4):
                                idx = (k * NCB + cb) * 4 + piece
                                if idx >= 3:
                                    sync.wait_ge(s_wod.h, idx - 2)
                                sync.dma_start(
                                    out=wobf[idx % 3].ap().rearrange("p (c j) -> p c j", j=SB),
                                    in_=woq[piece * 4:(piece + 1) * 4, :,
                                            cb * SB:(cb + 1) * SB].rearrange(
                                        "c p j -> p c j")).then_inc(s_wol.h[idx % 3], 16)
                                s_wol.inc(idx % 3)
                        for ss in range(4):
                            if s3_mode in ('ffn2noev', 'ffn2nx', 'ffn2dummy'):
                                continue
                            gidx = (k * NCB + cb) * 4 + ss
                            if gidx >= 4:
                                sync.wait_ge(s_oev.h, gidx - 3)
                            r0 = k * SB + ss * 128
                            sync.dma_start(out=xbb[ss][:],
                                           in_=x_in[r0:r0 + 128,
                                                    cb * SB:(cb + 1) * SB]).then_inc(s_xbl.h[ss], 16)
                            s_xbl.inc(ss)

                sync.wait_ge(s_xh.h, xh_done)
                xnt_load(0)
                w1_loads(0)
                for k in range(NSB):
                    if k + 1 < NSB:
                        sync.wait_ge(s_hm.h, k * FC)
                        xnt_load(k + 1)
                    cb_streams(k)
                    if k + 1 < NSB:
                        w1_loads(k + 1)

            @block.tensor
            def _(t):
                t.wait_ge(s_bb.h, bb_done)
                for k in range(NSB):
                    t.wait_ge(s_xnl.h[k % 2], 16 * (k // 2 + 1))
                    if k >= 1 and s3_mode in ('ffn2', 'ffn2nx', 'ffn2nw', 'ffn2nog', 'ffn2dummy', 'full'):
                        t.wait_ge(s_oev.h, (4 * k - 1) * 4)
                    for fs in range(NFS):
                        i0 = k * 32 + fs * 2
                        t.wait_ge(s_w1l.h[i0 % 4], 16 * (i0 // 4 + 1))
                        t.wait_ge(s_w1l.h[(i0 + 1) % 4], 16 * ((i0 + 1) // 4 + 1))
                        for f4 in range(4):
                            fc = fs * 4 + f4
                            bank = BK(fc % 2)
                            if fc >= 2:
                                t.wait_ge(s_gel.h, k * FC + fc - 1)
                            for cc in range(CC):
                                half = cc // 8
                                idx = k * 32 + fs * 2 + half
                                wsl = w1b[idx % 4]
                                ccl = cc % 8
                                mm = t.matmul(bank,
                                              wsl[:, ccl * SB + f4 * 128: ccl * SB + (f4 + 1) * 128],
                                              xnTb[k % 2][:, cc * SB:(cc + 1) * SB],
                                              start=(cc == 0), stop=(cc == CC - 1))
                            mm.then_inc(s_hm.h, 1)
                            s_hm.inc()
                        t.matmul(BK((fs * 4 + 3) % 2)[:, 0:128], p_zeros[:],
                                 p_zeros[:], start=False, stop=True,
                                 skip_group_check=True).then_inc(s_w1d.h, 2)
                        s_w1d.inc(2)
                    if s3_mode == 'ffn1':
                        continue
                    t.wait_ge(s_gel.h, (k + 1) * FC)
                    for cb in range(NCB):
                        banks = [BK((cb % 2) * 4 + ss) for ss in range(4)]
                        if k * NCB + cb >= 2 and s3_mode in ('ffn2', 'ffn2nx', 'ffn2nw', 'ffn2nog', 'ffn2dummy', 'full'):
                            t.wait_ge(s_oev.h, (k * NCB + cb - 1) * 4)
                        for piece in range(8):
                            idx = (k * NCB + cb) * 8 + piece
                            t.wait_ge(s_w2l.h[idx % 3], 16 * (idx // 3 + 1))
                            for fcl in range(8):
                                fc = piece * 8 + fcl
                                for ss in range(4):
                                    mm = t.matmul(banks[ss],
                                                  hT[:, fc * SB + ss * 128: fc * SB + (ss + 1) * 128],
                                                  w2b[idx % 3][:, fcl * SB:(fcl + 1) * SB],
                                                  start=(fc == 0), stop=False)
                            mm.then_inc(s_w2d.h, 1)
                            s_w2d.inc()
                        if s3_mode == 'full':
                            if cb == 0:
                                t.wait_ge(s_ul.h, 16 * (k + 1))
                            for piece in range(4):
                                idx = (k * NCB + cb) * 4 + piece
                                t.wait_ge(s_wol.h[idx % 3], 16 * (idx // 3 + 1))
                                for dcl in range(4):
                                    dc = piece * 4 + dcl
                                    for ss in range(4):
                                        mm = t.matmul(banks[ss],
                                                      ub3[:, dc * SB + ss * 128: dc * SB + (ss + 1) * 128],
                                                      wobf[idx % 3][:, dcl * SB:(dcl + 1) * SB],
                                                      start=False, stop=(dc == CC - 1))
                                if piece == 3:
                                    mm.then_inc(s_om.h, 4)
                                    s_om.inc(4)
                                    t.matmul(banks[3][:, 0:128], p_zeros[:],
                                             p_zeros[:], start=False, stop=True,
                                             skip_group_check=True).then_inc(s_wod.h, 1)
                                    s_wod.inc()
                                else:
                                    mm.then_inc(s_wod.h, 1)
                                    s_wod.inc()
                        else:
                            for ss in range(4):
                                mm = t.matmul(banks[ss],
                                              hT[:, ss * 128:(ss + 1) * 128],
                                              w2b[((k * NCB + cb) * 8 + 7) % 3][:, 0:SB],
                                              start=False, stop=True)
                            mm.then_inc(s_om.h, 4)
                            s_om.inc(4)

            @block.scalar
            def _(s):
                s.wait_ge(s_b2.h, b2v)
                for k in range(NSB):
                    for fc in range(FC):
                        s.wait_ge(s_hm.h, k * FC + fc + 1)
                        s.activation(hT[:, fc * SB:(fc + 1) * SB], BK(fc % 2),
                                     AF.Gelu, bias=p_b1T[:, fc:fc + 1]).then_inc(s_gel.h, 1)
                        s_gel.inc()


            @block.vector
            def _(v):
                v.wait_ge(s_bb.h, bb_done)
                for k in range(NSB):
                    if s3_mode in ('ffn1', 'ffn2noev'):
                        continue
                    for cb in range(NCB):
                        for ss in range(4):
                            gidx = (k * NCB + cb) * 4 + ss
                            v.wait_ge(s_om.h, (k * NCB + cb) * 4 + 4)
                            if s3_mode not in ('ffn2nx', 'ffn2dummy'):
                                v.wait_ge(s_xbl.h[ss], 16 * (k * NCB + cb + 1))
                            if gidx >= 2 and s3_mode not in ('ffn2nog', 'ffn2dummy'):
                                v.wait_ge(s_ow.h[gidx % 2], 16 * (gidx // 2))
                            bank = BK((cb % 2) * 4 + ss)
                            if s3_mode == 'ffn2dummy':
                                v.tensor_copy(outb[gidx % 2][:],
                                              bob2[:, cb * SB:(cb + 1) * SB]).then_inc(s_oev.h, 1)
                                s_oev.inc()
                            elif s3_mode != 'ffn2nx':
                                v.tensor_tensor(outb[gidx % 2][:], bank, xbb[ss][:], ALU.add)
                                v.tensor_tensor(outb[gidx % 2][:], outb[gidx % 2][:],
                                                bob2[:, cb * SB:(cb + 1) * SB],
                                                ALU.add).then_inc(s_oev.h, 1)
                            else:
                                v.tensor_tensor(outb[gidx % 2][:], bank[:],
                                                bob2[:, cb * SB:(cb + 1) * SB],
                                                ALU.add).then_inc(s_oev.h, 1)
                            s_oev.inc()

            @block.gpsimd
            def _(g):
                g.wait_ge(s_b2.h, b2v)

                def out_writes(k, cb):
                    if s3_mode in ('ffn1', 'ffn2noev', 'ffn2nog', 'ffn2dummy'):
                        return
                    for ss in range(4):
                        gidx = (k * NCB + cb) * 4 + ss
                        g.wait_ge(s_oev.h, gidx + 1)
                        r0 = k * SB + ss * 128
                        _dst = out_scr if s3_mode == 'ffn2nw' else out_hbm
                        g.dma_start(out=_dst[r0:r0 + 128, cb * SB:(cb + 1) * SB],
                                    in_=outb[gidx % 2][:]).then_inc(s_ow.h[gidx % 2], 16)
                        s_ow.inc(gidx % 2)

                def w2_stream(k, cb):
                    for piece in range(8):
                        idx = (k * NCB + cb) * 8 + piece
                        if idx >= 3:
                            g.wait_ge(s_w2d.h, idx - 2)
                        g.dma_start(
                            out=w2b[idx % 3].ap().rearrange("p (c j) -> p c j", j=SB),
                            in_=w2q[cb, piece * 8:(piece + 1) * 8, :, :].rearrange(
                                "c p j -> p c j")).then_inc(s_w2l.h[idx % 3], 16)
                        s_w2l.inc(idx % 3)

                def u_load_g(k):
                    if k >= 1:
                        g.wait_ge(s_om.h, k * 16)
                    g.dma_start(out=ub3.ap().rearrange("p (h j) -> p h j", j=SB),
                                in_=u_sc[:, k, :, :].rearrange("h p j -> p h j")).then_inc(s_ul.h, 16)
                    s_ul.inc(16)

                if s3_mode == 'full':
                    for i in range(2):
                        g.wait_ge(s_cw.h[i], cache_done[i])
                    seq = [(k, cb) for k in range(NSB) for cb in range(NCB)]
                    u_load_g(0)
                    w2_stream(*seq[0])
                    for j in range(1, len(seq)):
                        if seq[j][1] == 0:
                            u_load_g(seq[j][0])
                        w2_stream(*seq[j])
                        out_writes(*seq[j - 1])
                    out_writes(*seq[-1])
                else:
                    for k in range(NSB):
                        if s3_mode == 'ffn1':
                            continue
                        for cb in range(NCB):
                            out_writes(k, cb)
                if s3_mode == 'ffn1':
                    g.wait_ge(s_gel.h, NSB * FC)
                if s3_mode == 'ffn2noev':
                    g.wait_ge(s_om.h, NSB * 16)
                if s3_mode in ('ffn2nog', 'ffn2dummy'):
                    g.wait_ge(s_oev.h, NSB * NCB * 4)
                if debug:
                    g.dma_start(out=dbg_ht, in_=hT[:]).then_inc(s_owd.h, 16)
                    s_owd.inc(16)
                    g.dma_start(out=dbg_xnt, in_=xnTb[(NSB - 1) % 2][:]).then_inc(s_owd.h, 16)
                    s_owd.inc(16)
                    g.dma_start(out=dbg_ub3, in_=ub3[:]).then_inc(s_owd.h, 16)
                    s_owd.inc(16)
                if debug:
                    g.dma_start(out=dbg_xn, in_=xn_sc).then_inc(s_owd.h, 16)
                    s_owd.inc(16)
                    g.dma_start(out=dbg_u.rearrange("h s p j -> (h s) p j"),
                                in_=u_sc.rearrange("h s p j -> (h s) p j")).then_inc(s_owd.h, 16)
                    s_owd.inc(16)
                    g.dma_start(out=dbg_w1q.rearrange("a b p j -> (a b) p j"),
                                in_=w1q.rearrange("a b p j -> (a b) p j")).then_inc(s_owd.h, 16)
                    s_owd.inc(16)
                    g.dma_start(out=dbg_w2q.rearrange("a b p j -> (a b) p j"),
                                in_=w2q.rearrange("a b p j -> (a b) p j")).then_inc(s_owd.h, 16)
                    s_owd.inc(16)
                g.wait_ge(s_ow.h[0], s_ow.n[0])
                g.wait_ge(s_ow.h[1], s_ow.n[1])
                if s_owd.n:
                    g.wait_ge(s_owd.h, s_owd.n)

    return nc


def kernel(L0_attn_x, L0_attn_Wq, L0_attn_bq, L0_attn_Wk, L0_attn_bk,
           L0_attn_Wv, L0_attn_bv, L0_attn_Wo, L0_attn_bo,
           L0_attn_spiral_weight, L0_attn_helix_rotation,
           L0_ffn_w1, L0_ffn_b1, L0_ffn_w2, L0_ffn_b2):
    f32 = np.float32
    x = np.ascontiguousarray(np.asarray(L0_attn_x, f32))
    wq = np.asarray(L0_attn_Wq, f32); wk = np.asarray(L0_attn_Wk, f32)
    wv = np.asarray(L0_attn_Wv, f32); wo = np.asarray(L0_attn_Wo, f32)
    w1 = np.asarray(L0_ffn_w1, f32); w2 = np.asarray(L0_ffn_w2, f32)

    shared = dict(
        wqT=np.ascontiguousarray(wq.T),
        wkT=np.ascontiguousarray(wk.T),
        wvT=np.ascontiguousarray(wv.T),
        woT=np.ascontiguousarray(wo.T),
        w1s=np.ascontiguousarray(w1.T.reshape(CC, 128, NFS, 512).transpose(2, 0, 1, 3)),
        w2s=np.ascontiguousarray(w2.T.reshape(FC, 128, NCB, 512).transpose(2, 0, 1, 3)),
        bqT=np.ascontiguousarray(np.asarray(L0_attn_bq, f32).reshape(H, 128).T),
        bkT=np.ascontiguousarray(np.asarray(L0_attn_bk, f32).reshape(H, 128).T),
        rotT=np.ascontiguousarray(np.asarray(L0_attn_helix_rotation, f32).reshape(H, 128).T),
        spT=np.ascontiguousarray(np.broadcast_to(np.asarray(L0_attn_spiral_weight, f32), (128, H))),
        b1T=np.ascontiguousarray(np.asarray(L0_ffn_b1, f32).reshape(FC, 128).T),
        bo_rep=np.ascontiguousarray(np.broadcast_to(np.asarray(L0_attn_bo, f32), (128, D))),
        b2_rep=np.ascontiguousarray(np.broadcast_to(np.asarray(L0_ffn_b2, f32), (128, D))),
        bvT=np.ascontiguousarray(np.asarray(L0_attn_bv, f32).reshape(H, 128).T),
    )
    in_maps = [dict(x=np.ascontiguousarray(x[b]), **shared) for b in range(8)]
    nc = build()
    res = run_bass_kernel_spmd(nc, in_maps, core_ids=list(range(8)))
    return np.stack([np.asarray(res.results[b]["out"], f32) for b in range(8)], axis=0)


if __name__ == "__main__":
    build()
    print("built ok")


# version 26
# speedup vs baseline: 1.0317x; 1.0317x over previous
import numpy as np
import ml_dtypes
from contextlib import ExitStack

import concourse.bass as bass
import concourse.mybir as mybir
from concourse.bass_utils import run_bass_kernel_spmd

F32 = mybir.dt.float32
BF16 = mybir.dt.bfloat16
AF = mybir.ActivationFunctionType
ALU = mybir.AluOpType
AX = mybir.AxisListType

S, D, H, DH, DFF = 4096, 2048, 16, 128, 8192
CC = D // 128
TC = S // 128
FC = DFF // 128
SB = 512
NSB = S // SB
NFS = 16
NCB = 4
INV_SQRT_DH = float(1.0 / np.sqrt(128.0))
LN_EPS = 1e-5
TWO_PI = float(2.0 * np.pi)
RND_MAGIC = 12582912.0
W_COUNT = float(DFF * D)


class Ctr:
    def __init__(self, stack, nc, name):
        self.h = stack.enter_context(nc.semaphore(name))
        self.n = 0

    def inc(self, k=1):
        self.n += k
        return self.n


class SlotCtr:
    def __init__(self, stack, nc, name, nslots):
        self.h = [stack.enter_context(nc.semaphore(f"{name}{i}")) for i in range(nslots)]
        self.n = [0] * nslots

    def inc(self, slot, k=16):
        self.n[slot] += k
        return self.n[slot]


def build(debug=False, last_phase=3, s3_mode='full'):
    nc = bass.Bass("TRN2", target_bir_lowering=False)

    x_in = nc.dram_tensor("x", [S, D], F32, kind="ExternalInput").ap()
    wqT = nc.dram_tensor("wqT", [D, D], F32, kind="ExternalInput").ap()
    wkT = nc.dram_tensor("wkT", [D, D], F32, kind="ExternalInput").ap()
    wvT = nc.dram_tensor("wvT", [D, D], F32, kind="ExternalInput").ap()
    woT = nc.dram_tensor("woT", [D, D], F32, kind="ExternalInput").ap()
    w1s = nc.dram_tensor("w1s", [NFS, CC, 128, 512], F32, kind="ExternalInput").ap()
    w2s = nc.dram_tensor("w2s", [NCB, FC, 128, 512], F32, kind="ExternalInput").ap()
    bqT_in = nc.dram_tensor("bqT", [128, H], F32, kind="ExternalInput").ap()
    bkT_in = nc.dram_tensor("bkT", [128, H], F32, kind="ExternalInput").ap()
    rotT_in = nc.dram_tensor("rotT", [128, H], F32, kind="ExternalInput").ap()
    spT_in = nc.dram_tensor("spT", [128, H], F32, kind="ExternalInput").ap()
    b1T_in = nc.dram_tensor("b1T", [128, FC], F32, kind="ExternalInput").ap()
    bo_rep = nc.dram_tensor("bo_rep", [128, D], F32, kind="ExternalInput").ap()
    b2_rep = nc.dram_tensor("b2_rep", [128, D], F32, kind="ExternalInput").ap()
    bvT_in = nc.dram_tensor("bvT", [128, H], F32, kind="ExternalInput").ap()

    out_hbm = nc.dram_tensor("out", [S, D], F32, kind="ExternalOutput").ap()

    xn_sc = nc.dram_tensor("xn_sc", [S, D], BF16, kind="Internal").ap()
    u_sc = nc.dram_tensor("u_sc", [H, NSB, 128, SB], BF16, kind="Internal").ap()
    w1q = nc.dram_tensor("w1q", [NFS, CC, 128, 512], BF16, kind="Internal").ap()
    w2q = nc.dram_tensor("w2q", [NCB, FC, 128, 512], BF16, kind="Internal").ap()
    woq = nc.dram_tensor("woq", [CC, 128, D], BF16, kind="Internal").ap()
    out_scr = nc.dram_tensor("out_scr", [S, D], F32, kind="Internal").ap()
    xnT_hbm = nc.dram_tensor("xnT_hbm", [CC, 128, S], BF16, kind="Internal").ap()

    if debug:
        dbg_ht = nc.dram_tensor("dbg_ht", [128, FC * SB], BF16, kind="ExternalOutput").ap()
        dbg_xnt = nc.dram_tensor("dbg_xnt", [128, CC * SB], BF16, kind="ExternalOutput").ap()
        dbg_ub3 = nc.dram_tensor("dbg_ub3", [128, H * SB], BF16, kind="ExternalOutput").ap()
        dbg_xn = nc.dram_tensor("dbg_xn", [S, D], BF16, kind="ExternalOutput").ap()
        dbg_u = nc.dram_tensor("dbg_u", [H, NSB, 128, SB], BF16, kind="ExternalOutput").ap()
        dbg_w1q = nc.dram_tensor("dbg_w1q", [NFS, CC, 128, 512], BF16, kind="ExternalOutput").ap()
        dbg_w2q = nc.dram_tensor("dbg_w2q", [NCB, FC, 128, 512], BF16, kind="ExternalOutput").ap()

    ones_bf = nc.inline_tensor(np.ones((128, 128), dtype=ml_dtypes.bfloat16), "ones_bf").ap()
    ones_r = nc.inline_tensor(np.ones((1, 128), dtype=np.float32), "ones_r").ap()
    ones_c = nc.inline_tensor(np.ones((128, 1), dtype=np.float32), "ones_c").ap()
    zeros_b = nc.inline_tensor(np.zeros((128, 128), dtype=ml_dtypes.bfloat16), "zeros_b").ap()
    ident_b = nc.inline_tensor(np.eye(128, dtype=ml_dtypes.bfloat16), "ident_b").ap()

    with ExitStack() as root:
        e = root.enter_context

        p_bqT = e(nc.sbuf_tensor("p_bqT", [128, H], F32))
        p_bkT = e(nc.sbuf_tensor("p_bkT", [128, H], F32))
        p_bvT = e(nc.sbuf_tensor("p_bvT", [128, H], F32))
        p_qsc = e(nc.sbuf_tensor("p_qsc", [128, H], F32))
        p_ksc = e(nc.sbuf_tensor("p_ksc", [128, H], F32))
        p_bqe = e(nc.sbuf_tensor("p_bqe", [128, H], F32))
        p_bke = e(nc.sbuf_tensor("p_bke", [128, H], F32))
        p_b1T = e(nc.sbuf_tensor("p_b1T", [128, FC], F32))
        p_ones = e(nc.sbuf_tensor("p_ones", [128, 128], BF16))
        p_onesr = e(nc.sbuf_tensor("p_onesr", [1, 128], F32))
        p_onesc = e(nc.sbuf_tensor("p_onesc", [128, 1], F32))
        p_zeros = e(nc.sbuf_tensor("p_zeros", [128, 128], BF16))
        p_identb = e(nc.sbuf_tensor("p_identb", [128, 128], BF16))
        p_thr1 = e(nc.sbuf_tensor("p_thr1", [128, 1], F32))
        p_thr2 = e(nc.sbuf_tensor("p_thr2", [128, 1], F32))
        p_nthr1 = e(nc.sbuf_tensor("p_nthr1", [128, 1], F32))
        p_nthr2 = e(nc.sbuf_tensor("p_nthr2", [128, 1], F32))
        p_t1 = e(nc.sbuf_tensor("p_t1", [128, H], F32))
        p_t2 = e(nc.sbuf_tensor("p_t2", [128, H], F32))
        p_t3 = e(nc.sbuf_tensor("p_t3", [128, H], F32))
        p_thrs = e(nc.sbuf_tensor("p_thrs", [1, 2], F32))
        p_eps = e(nc.sbuf_tensor("p_eps", [128, 1], F32))
        p_spc = e(nc.sbuf_tensor("p_spc", [128, 512], F32))

        def spacer(v):
            v.tensor_copy(p_spc[:, 0:512], p_spc[:, 0:512])

        pbA = e(nc.psum_tensor("pbA", [128, 1024], F32))
        pb_mid = [e(nc.psum_tensor(f"pbm{i}", [128, 512], F32)) for i in range(4)]
        pbB = e(nc.psum_tensor("pbB", [128, 1024], F32))

        def BK(i):
            if i == 0:
                return pbA[:, 0:512]
            if i == 1:
                return pbA[:, 512:1024]
            if i == 6:
                return pbB[:, 0:512]
            if i == 7:
                return pbB[:, 512:1024]
            return pb_mid[i - 2][:]

        s_p = Ctr(root, nc, "s_p")
        s_pc = Ctr(root, nc, "s_pc")
        s_pe0 = Ctr(root, nc, "s_pe0")
        s_ld0 = SlotCtr(root, nc, "s_ld0", 2)
        s_dq = Ctr(root, nc, "s_dq")
        s_sg = Ctr(root, nc, "s_sg")
        s_cw = SlotCtr(root, nc, "s_cw", 2)
        s_woc = Ctr(root, nc, "s_woc")
        s_x = SlotCtr(root, nc, "s_x", 4)
        s_ln = Ctr(root, nc, "s_ln")
        s_lna = Ctr(root, nc, "s_lna")
        s_xnw = SlotCtr(root, nc, "s_xnw", 2)
        s_b1 = Ctr(root, nc, "s_b1")
        s_thr = Ctr(root, nc, "s_thr")
        s_xt = Ctr(root, nc, "s_xt")
        s_tp = Ctr(root, nc, "s_tp")
        s_xnT = Ctr(root, nc, "s_xnT")
        s_wv = Ctr(root, nc, "s_wv")
        s_wqk = Ctr(root, nc, "s_wqk")
        s_pvm = Ctr(root, nc, "s_pvm")
        s_vev = Ctr(root, nc, "s_vev")
        s_qkm = Ctr(root, nc, "s_qkm")
        s_qke = Ctr(root, nc, "s_qke")
        s_stm = Ctr(root, nc, "s_stm")
        s_exp = Ctr(root, nc, "s_exp")
        s_avd = Ctr(root, nc, "s_avd")
        s_nrm = Ctr(root, nc, "s_nrm")
        s_uw = SlotCtr(root, nc, "s_uw", 2)
        s_b2 = Ctr(root, nc, "s_b2")
        s_bb = Ctr(root, nc, "s_bb")
        s_ul = Ctr(root, nc, "s_ul")
        s_xh = Ctr(root, nc, "s_xh")
        s_xnl = SlotCtr(root, nc, "s_xnl", 2)
        s_w1l = SlotCtr(root, nc, "s_w1l", 4)
        s_w1d = Ctr(root, nc, "s_w1d")
        s_w2l = SlotCtr(root, nc, "s_w2l", 3)
        s_w2d = Ctr(root, nc, "s_w2d")
        s_wol = SlotCtr(root, nc, "s_wol", 4)
        s_wod = Ctr(root, nc, "s_wod")
        s_xbl = SlotCtr(root, nc, "s_xbl", 4)
        s_hm = Ctr(root, nc, "s_hm")
        s_gel = Ctr(root, nc, "s_gel")
        s_om = Ctr(root, nc, "s_om")
        s_oev = Ctr(root, nc, "s_oev")
        s_ow = SlotCtr(root, nc, "s_ow", 2)
        s_owd = Ctr(root, nc, "s_owd")

        block = e(nc.Block())

        @block.sync
        def _(sync):
            for dst, src in [(p_bqT, bqT_in), (p_bkT, bkT_in), (p_bvT, bvT_in),
                             (p_t1, rotT_in),
                             (p_t2, spT_in), (p_b1T, b1T_in)]:
                sync.dma_start(out=dst[:], in_=src).then_inc(s_p.h, 16)
                s_p.inc(16)
            for dst, src in [(p_ones, ones_bf), (p_onesr, ones_r), (p_onesc, ones_c),
                             (p_zeros, zeros_b), (p_identb, ident_b)]:
                sync.dma_start(out=dst[:], in_=src).then_inc(s_p.h, 16)
                s_p.inc(16)
        p_loads = s_p.n

        @block.vector
        def _(v):
            v.wait_ge(s_p.h, p_loads)
            v.memset(p_eps[:], LN_EPS)
            v.tensor_scalar(p_t3[:], p_t1[:], 1.0 / TWO_PI, RND_MAGIC, ALU.mult, ALU.add)
            spacer(v)
            v.tensor_scalar(p_t3[:], p_t3[:], RND_MAGIC, None, ALU.subtract)
            spacer(v)
            v.scalar_tensor_tensor(p_ksc[:], p_t3[:], -TWO_PI, p_t1[:],
                                   ALU.mult, ALU.add)
            spacer(v)
            v.tensor_copy(p_spc[:, 0:16], p_ksc[:]).then_inc(s_pc.h, 1)
            s_pc.inc()
        rr_done = s_pc.n

        @block.scalar
        def _(s):
            s.wait_ge(s_pc.h, rr_done)
            s.activation(p_t3[:], p_ksc[:], AF.Sin, scale=0.5).then_inc(s_sg.h, 1)
            s_sg.inc()
            s.activation(p_t1[:], p_ksc[:], AF.Sin).then_inc(s_sg.h, 1)
            s_sg.inc()
        trig_acts = s_sg.n

        @block.vector
        def _(v):
            v.wait_ge(s_sg.h, trig_acts)
            v.tensor_tensor(p_qsc[:], p_t3[:], p_t3[:], ALU.mult)
            spacer(v)
            v.tensor_scalar(p_qsc[:], p_qsc[:], -2.0, 1.0, ALU.mult, ALU.add)
            spacer(v)
            v.tensor_tensor(p_qsc[:], p_qsc[:], p_t2[:], ALU.mult)
            spacer(v)
            v.tensor_scalar(p_qsc[:], p_qsc[:], INV_SQRT_DH, None, ALU.mult)
            spacer(v)
            v.tensor_copy(p_ksc[:], p_t1[:])
            spacer(v)
            v.tensor_tensor(p_bqe[:], p_bqT[:], p_qsc[:], ALU.mult)
            spacer(v)
            v.tensor_tensor(p_bke[:], p_bkT[:], p_ksc[:], ALU.mult)
            spacer(v)
            v.tensor_copy(p_spc[:, 0:16], p_qsc[:]).then_inc(s_pc.h, 1)
            s_pc.inc()
        trig_done = s_pc.n

        s12 = ExitStack()
        xnT = s12.enter_context(nc.sbuf_tensor("xnT", [128, CC * S], BF16))

        with ExitStack() as s01:
            e01 = s01.enter_context
            xti = [e01(nc.sbuf_tensor(f"xti{i}", [128, D], F32)) for i in range(4)]
            xnb = [e01(nc.sbuf_tensor(f"xnb{i}", [128, D], BF16)) for i in range(2)]
            lnscr = [e01(nc.sbuf_tensor(f"lnscr{i}", [128, D], BF16)) for i in range(2)]
            st_sums = e01(nc.sbuf_tensor("st_sums", [128, TC], F32))
            st_sumsq = e01(nc.sbuf_tensor("st_sumsq", [128, TC], F32))
            st_nmean = e01(nc.sbuf_tensor("st_nmean", [128, TC], F32))
            st_nm2 = e01(nc.sbuf_tensor("st_nm2", [128, TC], F32))
            st_msq = e01(nc.sbuf_tensor("st_msq", [128, TC], F32))
            st_var = e01(nc.sbuf_tensor("st_var", [128, TC], F32))
            st_sqv = e01(nc.sbuf_tensor("st_sqv", [128, TC], F32))
            st_rstd = e01(nc.sbuf_tensor("st_rstd", [128, TC], F32))

            def x_loads(eng, par):
                for i in range(par, TC, 2):
                    if i >= 4:
                        eng.wait_ge(s_ln.h, 2 * (i - 3))
                        eng.wait_ge(s_lna.h, i - 3)
                    eng.dma_start(out=xti[i % 4][:],
                                  in_=x_in[i * 128:(i + 1) * 128, :]).then_inc(s_x.h[i % 4], 16)
                    s_x.inc(i % 4)
                for i in range(par, TC, 2):
                    if i < 4:
                        eng.wait_ge(s_ln.h, 2 * (TC - 4 + i) + 2)
                        eng.wait_ge(s_lna.h, TC - 4 + i + 1)
                    else:
                        eng.wait_ge(s_ln.h, 66 + i - 3)
                    eng.dma_start(out=xti[i % 4][:],
                                  in_=x_in[i * 128:(i + 1) * 128, :]).then_inc(s_x.h[i % 4], 16)
                    s_x.inc(i % 4)

            @block.sync
            def _(sync):
                x_loads(sync, 0)

            @block.gpsimd
            def _(g):
                x_loads(g, 1)

            @block.scalar
            def _(s):
                for i in range(TC):
                    s.wait_ge(s_x.h[i % 4], 16 * (i // 4 + 1))
                    if i >= 2:
                        s.wait_ge(s_ln.h, 2 * (i - 1))
                    s.activation(lnscr[i % 2][:], xti[i % 4][:],
                                 AF.Square).then_inc(s_lna.h, 1)
                    s_lna.inc()
                s.wait_ge(s_ln.h, 65)
                s.activation(st_sqv[:], st_var[:], AF.Sqrt,
                             bias=p_eps[:, 0:1]).then_inc(s_lna.h, 1)
                s_lna.inc()
            sqv_done = s_lna.n

            @block.vector
            def _(v):
                for i in range(TC):
                    v.wait_ge(s_x.h[i % 4], 16 * (i // 4 + 1))
                    v.reduce_sum(st_sums[:, i:i + 1], xti[i % 4][:],
                                 AX.X).then_inc(s_ln.h, 1)
                    v.wait_ge(s_lna.h, i + 1)
                    v.reduce_sum(st_sumsq[:, i:i + 1], lnscr[i % 2][:],
                                 AX.X).then_inc(s_ln.h, 1)
                spacer(v)
                v.tensor_scalar(st_nmean[:], st_sums[:], -1.0 / D, None, ALU.mult)
                spacer(v)
                v.tensor_tensor(st_nm2[:], st_nmean[:], st_nmean[:], ALU.mult)
                v.tensor_scalar(st_msq[:], st_sumsq[:], 1.0 / D, None, ALU.mult)
                spacer(v)
                v.tensor_tensor(st_var[:], st_msq[:], st_nm2[:], ALU.subtract)
                spacer(v)
                v.tensor_copy(p_spc[:, 0:32], st_var[:]).then_inc(s_ln.h, 1)
                s_ln.inc(2 * TC + 1)
                v.wait_ge(s_lna.h, sqv_done)
                v.reciprocal(st_rstd[:], st_sqv[:])
                spacer(v)
                v.tensor_copy(p_spc[:, 0:32], st_rstd[:]).then_inc(s_ln.h, 1)
                s_ln.inc(1)
                for i in range(TC):
                    v.wait_ge(s_x.h[i % 4], 16 * (TC // 4 + i // 4 + 1))
                    if i >= 2:
                        v.wait_ge(s_tp.h, (i - 1) * CC)
                    v.tensor_scalar(xnb[i % 2][:], xti[i % 4][:],
                                    st_nmean[:, i:i + 1], st_rstd[:, i:i + 1],
                                    ALU.add, ALU.mult).then_inc(s_ln.h, 1)
                    s_ln.inc(1)

            @block.tensor
            def _(t):
                t.wait_ge(s_p.h, p_loads)
                pAb = pbA.bitcast(BF16)
                for i in range(TC):
                    t.wait_ge(s_ln.h, 66 + i + 1)
                    for c in range(CC):
                        j = i * CC + c
                        if j >= 2:
                            t.wait_ge(s_xt.h, j - 1)
                        t.transpose(pAb[:, (j % 2) * 1024:(j % 2) * 1024 + 128],
                                    xnb[i % 2][:, c * 128:(c + 1) * 128],
                                    p_identb[:]).then_inc(s_tp.h, 1)
                        s_tp.inc()

            @block.scalar
            def _(s):
                for i in range(TC):
                    for c in range(CC):
                        j = i * CC + c
                        s.wait_ge(s_tp.h, j + 1)
                        s.copy(xnT[:, c * S + i * 128: c * S + (i + 1) * 128],
                               pbA.bitcast(BF16)[:, (j % 2) * 1024:(j % 2) * 1024 + 128]
                               ).then_inc(s_xt.h, 1)
                        s_xt.inc()
            xt_done = s_xt.n

        @block.sync
        def _(sync):
            sync.wait_ge(s_xt.h, xt_done)
            sync.nop().then_inc(s_b1.h, 1)
            s_b1.inc()
        b1v = s_b1.n

        def hs_of(h, sb):
            return h * NSB + sb

        def cnt(hs, tc):
            return hs * TC + tc + 1

        VB = (0, 1, 6, 7)

        def pcnt(hs):
            return (hs + 1) * (TC // 2)

        if last_phase < 2:
            @block.gpsimd
            def _(g):
                g.wait_ge(s_xt.h, xt_done)
            return nc

        NPC = 512

        with ExitStack() as s2:
            e2 = s2.enter_context
            vg = e2(nc.sbuf_tensor("vg", [128, TC * 256], BF16))
            wvb = e2(nc.sbuf_tensor("wvb", [128, CC * 256], BF16))
            wqb = e2(nc.sbuf_tensor("wqb", [128, CC * 128], BF16))
            wkb = e2(nc.sbuf_tensor("wkb", [128, CC * 128], BF16))
            qt = e2(nc.sbuf_tensor("qt", [128, S], BF16))
            kt = e2(nc.sbuf_tensor("kt", [128, S], BF16))
            eb = [e2(nc.sbuf_tensor(f"eb{i}", [128, 2 * SB], BF16)) for i in range(2)]
            recb = e2(nc.sbuf_tensor("recb", [128, SB], F32))
            ub = [e2(nc.sbuf_tensor(f"ub{i}", [128, SB], BF16)) for i in range(2)]
            slabp = [e2(nc.sbuf_tensor(f"slabp{i}", [128, 512], F32)) for i in range(2)]
            qbq = [e2(nc.sbuf_tensor(f"qbq{i}", [128, 512], BF16)) for i in range(2)]
            redrs = e2(nc.sbuf_tensor("redrs", [128, NPC], F32))
            accr = e2(nc.sbuf_tensor("accr", [128, 2], F32))
            wob = e2(nc.sbuf_tensor("wob", [128, D], BF16))

            w1f = w1s.rearrange("a b p j -> (a b) p j")
            w2f = w2s.rearrange("a b p j -> (a b) p j")
            w1qf = w1q.rearrange("a b p j -> (a b) p j")
            w2qf = w2q.rearrange("a b p j -> (a b) p j")

            def psrc(p):
                return w1f[p] if p < 256 else w2f[p - 256]

            def pdst(p):
                return w1qf[p] if p < 256 else w2qf[p - 256]

            SC_AT = {0: 86, 1: 86, 2: 86, 3: 86, 4: 86, 5: 82}
            QP_AT = {h: 64 for h in range(8, 16)}

            @block.sync
            def _(sync):
                sync.wait_ge(s_b1.h, b1v)
                sync.nop().then_inc(s_xnT.h, 16)
                s_xnT.inc(16)
            xnT_done = s_xnT.n

            @block.sync
            def _(sync):
                sync.wait_ge(s_xnT.h, xnT_done)
                for c in range(CC):
                    sync.dma_start(out=xnT_hbm[c, :, :],
                                   in_=xnT[:, c * S:(c + 1) * S]).then_inc(s_xh.h, 16)
                    s_xh.inc(16)
            xh_done = s_xh.n

            @block.sync
            def _(sync):
                for p in range(NPC):
                    if p >= 2:
                        sync.wait_ge(s_dq.h, p - 1)
                    sync.dma_start(out=slabp[p % 2][:],
                                   in_=psrc(p)).then_inc(s_ld0.h[p % 2], 16)
                    s_ld0.inc(p % 2)
                for p in range(NPC + 2):
                    if 2 <= p < NPC + 2:
                        sync.wait_ge(s_dq.h, 512 + p)
                    if p < NPC:
                        if p < 2:
                            sync.wait_ge(s_dq.h, 511 + p)
                        sync.dma_start(out=slabp[p % 2][:],
                                       in_=psrc(p)).then_inc(s_ld0.h[p % 2], 16)
                        s_ld0.inc(p % 2)
                    if p >= 2:
                        sync.dma_start(out=pdst(p - 2),
                                       in_=qbq[(p - 2) % 2][:]).then_inc(s_cw.h[(p - 2) % 2], 16)
                        s_cw.inc((p - 2) % 2)

            @block.gpsimd
            def _(g):
                g.wait_ge(s_b1.h, b1v)
                g.nop().then_inc(s_wv.h, 16)
                s_wv.inc(16)

                def u_writes_for_head(hh):
                    for sb in range(NSB):
                        hs = hs_of(hh, sb)
                        g.wait_ge(s_nrm.h, hs + 1)
                        g.dma_start(out=u_sc[hh, sb, :, :],
                                    in_=ub[hs % 2][:]).then_inc(s_uw.h[hs % 2], 16)
                        s_uw.inc(hs % 2)

                def g_inject(h):
                    if 1 <= h <= 4:
                        for dc in range(4 * (h - 1), 4 * h):
                            if dc >= 1:
                                g.wait_ge(s_woc.h, 32 * dc)
                            g.dma_start(out=wob[:],
                                        in_=woT[dc * 128:(dc + 1) * 128, :]).then_inc(s_woc.h, 16)
                            s_woc.inc(16)
                            g.wait_ge(s_woc.h, s_woc.n)
                            g.dma_start(out=woq[dc, :, :], in_=wob[:]).then_inc(s_woc.h, 16)
                            s_woc.inc(16)

                for h in range(H):
                    if h % 2 == 0:
                        g2 = h // 2
                        if g2 >= 1:
                            g.wait_ge(s_pvm.h, g2 * (TC // 2))
                        src = wvT.rearrange("(cc p) (g j) -> p cc g j", p=128, j=256)[:, :, g2, :]
                        g.dma_start(out=wvb.ap().rearrange("p (c j) -> p c j", j=256),
                                    in_=src).then_inc(s_wv.h, 16)
                        s_wv.inc(16)
                    if h >= 1:
                        g.wait_ge(s_qkm.h, h * 16)
                    srcq = wqT.rearrange("(cc p) (h j) -> p cc h j", p=128, j=128)[:, :, h, :]
                    srck = wkT.rearrange("(cc p) (h j) -> p cc h j", p=128, j=128)[:, :, h, :]
                    g.dma_start(out=wqb.ap().rearrange("p (c j) -> p c j", j=128),
                                in_=srcq).then_inc(s_wqk.h, 16)
                    s_wqk.inc(16)
                    g.dma_start(out=wkb.ap().rearrange("p (c j) -> p c j", j=128),
                                in_=srck).then_inc(s_wqk.h, 16)
                    s_wqk.inc(16)
                    if h >= 1:
                        u_writes_for_head(h - 1)
                    g_inject(h)
                u_writes_for_head(H - 1)
            u_writes = list(s_uw.n)
            wo_done = s_woc.n
            cache_done = list(s_cw.n)

            @block.tensor
            def _(t):
                t.wait_ge(s_xnT.h, xnT_done)

                def pe_inject(h):
                    if h == 7:
                        t.wait_ge(s_exp.h, pcnt(hs_of(6, NSB - 1)))
                        t.wait_ge(s_vev.h, 4 * (TC // 2))
                        t.wait_ge(s_dq.h, 513)
                        t.matmul(BK(0)[0:1, 0:2], p_onesc[:, 0:1], accr[:, 0:2],
                                 start=True, stop=True)
                        t.matmul(BK(0)[0:1, 0:128], p_zeros[:, 0:1], p_zeros[:],
                                 start=False, stop=True,
                                 skip_group_check=True).then_inc(s_pe0.h, 1)
                        s_pe0.inc()
                    if h == 8:
                        t.wait_ge(s_exp.h, pcnt(hs_of(7, NSB - 1)))
                        t.wait_ge(s_vev.h, 5 * (TC // 2))
                        t.wait_ge(s_thr.h, 1)
                        t.matmul(BK(0)[0:128, 0:1], p_onesr[0:1, :], p_thrs[0:1, 0:1],
                                 start=True, stop=True)
                        t.matmul(BK(1)[0:128, 0:1], p_onesr[0:1, :], p_thrs[0:1, 1:2],
                                 start=True, stop=True)
                        t.matmul(BK(1)[0:1, 0:128], p_zeros[:, 0:1], p_zeros[:],
                                 start=False, stop=True,
                                 skip_group_check=True).then_inc(s_pe0.h, 1)
                        s_pe0.inc()

                for g2 in range(8):
                    t.wait_ge(s_wv.h, 16 * (g2 + 2))
                    if g2 >= 1:
                        t.wait_ge(s_exp.h, pcnt(hs_of(2 * g2 - 1, NSB - 1)))
                    for tc in range(TC):
                        vp = tc // 2
                        half = tc % 2
                        gp = g2 * (TC // 2) + vp
                        bank = BK(VB[vp % 4])
                        if half == 0 and gp >= 4:
                            t.wait_ge(s_vev.h, gp - 3)
                        for cc in range(CC):
                            mm = t.matmul(bank[:, half * 256:(half + 1) * 256],
                                          xnT[:, cc * S + tc * 128: cc * S + (tc + 1) * 128],
                                          wvb[:, cc * 256:(cc + 1) * 256],
                                          start=(cc == 0), stop=(cc == CC - 1))
                        if half == 1:
                            mm.then_inc(s_pvm.h, 1)
                            s_pvm.inc()
                    for hl in range(2):
                        h = 2 * g2 + hl
                        pe_inject(h)
                        t.wait_ge(s_wqk.h, 32 * (h + 1))
                        if h >= 1:
                            t.wait_ge(s_exp.h, pcnt(hs_of(h - 1, NSB - 1)))
                        if hl == 0:
                            t.wait_ge(s_vev.h, (g2 + 1) * (TC // 2))
                        for qk in range(2):
                            wb = wqb if qk == 0 else wkb
                            for sb in range(NSB):
                                grp = h * 16 + qk * 8 + sb
                                bank = BK(6 + grp % 2)
                                if grp >= 2:
                                    t.wait_ge(s_qke.h, grp - 1)
                                for cc in range(CC):
                                    mm = t.matmul(bank,
                                                  wb[:, cc * 128:(cc + 1) * 128],
                                                  xnT[:, cc * S + sb * SB: cc * S + (sb + 1) * SB],
                                                  start=(cc == 0), stop=(cc == CC - 1))
                                mm.then_inc(s_qkm.h, 1)
                                s_qkm.inc()
                        if hl == 0:
                            t.wait_ge(s_vev.h, (g2 + 1) * (TC // 2))
                        if h == 7:
                            t.wait_ge(s_thr.h, 1)
                        if h == 8:
                            t.wait_ge(s_thr.h, 2)
                        t.wait_ge(s_qke.h, h * 16 + 16)
                        for sb in range(NSB):
                            hs = hs_of(h, sb)
                            ub_bank = BK(2 + hs % 2)
                            den_bank = BK(4 + hs % 2)
                            if hs >= 2:
                                t.wait_ge(s_nrm.h, hs - 1)

                            def st_pair(p, sbq=None):
                                sq = sb if sbq is None else sbq
                                pbank = [pbA, pbB][p % 2]
                                for half in range(2):
                                    tc = 2 * p + half
                                    t.matmul(pbank[:, half * 512:(half + 1) * 512],
                                             kt[:, tc * 128:(tc + 1) * 128],
                                             qt[:, sq * SB:(sq + 1) * SB],
                                             start=True, stop=True).then_inc(s_stm.h, 1)
                                    s_stm.inc()

                            NP = TC // 2
                            if sb == 0:
                                if hs >= 1:
                                    t.wait_ge(s_exp.h, hs * NP - 1)
                                st_pair(0)
                                if hs >= 1:
                                    t.wait_ge(s_exp.h, hs * NP)
                                st_pair(1)
                            for p in range(NP):
                                t.wait_ge(s_exp.h, hs * NP + p + 1)
                                if p >= NP - 2 and sb < NSB - 1:
                                    st_pair(p - (NP - 2), sb + 1)
                                for half in range(2):
                                    tc = 2 * p + half
                                    t.matmul(ub_bank,
                                             vg[:, tc * 256 + hl * 128: tc * 256 + (hl + 1) * 128],
                                             eb[p % 2][:, half * SB:(half + 1) * SB],
                                             start=(tc == 0), stop=(tc == TC - 1))
                                    t.matmul(den_bank,
                                             p_ones[:],
                                             eb[p % 2][:, half * SB:(half + 1) * SB],
                                             start=(tc == 0), stop=(tc == TC - 1)).then_inc(s_avd.h, 1)
                                    s_avd.inc()
                                if p + 2 < NP:
                                    st_pair(p + 2)
                t.matmul(BK(0)[:, 0:128], p_zeros[:], p_zeros[:],
                         start=False, stop=True,
                         skip_group_check=True).then_inc(s_b2.h, 1)
                s_b2.inc()

            @block.scalar
            def _(s):
                s.wait_ge(s_pc.h, trig_done)
                for h in range(H):
                    if h % 2 == 0:
                        g2 = h // 2
                        for vp in range(TC // 2):
                            s.wait_ge(s_pvm.h, g2 * (TC // 2) + vp + 1)
                            s.copy(vg[:, vp * 512:(vp + 1) * 512],
                                   BK(VB[vp % 4])).then_inc(s_vev.h, 1)
                            s_vev.inc()
                    for qk in range(2):
                        dstt = qt if qk == 0 else kt
                        bcol = p_bqe if qk == 0 else p_bke
                        scol = p_qsc if qk == 0 else p_ksc
                        for sb in range(NSB):
                            grp = h * 16 + qk * 8 + sb
                            s.wait_ge(s_qkm.h, grp + 1)
                            s.activation(dstt[:, sb * SB:(sb + 1) * SB],
                                         BK(6 + grp % 2), AF.Identity,
                                         bias=bcol[:, h:h + 1],
                                         scale=scol[:, h:h + 1]).then_inc(s_qke.h, 1)
                            s_qke.inc()
                    for sb in range(NSB):
                        hs = hs_of(h, sb)
                        for p in range(TC // 2):
                            s.wait_ge(s_stm.h, cnt(hs, 2 * p + 1))
                            if p >= 2:
                                s.wait_ge(s_avd.h, cnt(hs, 2 * p - 3))
                            elif hs >= 1:
                                s.wait_ge(s_avd.h, cnt(hs - 1, TC - 3 + 2 * p))
                            s.activation(eb[p % 2][:], [pbA, pbB][p % 2][:],
                                         AF.Exp).then_inc(s_exp.h, 1)
                            s_exp.inc()

            @block.vector
            def _(v):
                v.wait_ge(s_xnT.h, xnT_done)
                sc_done = [0]
                qp_done_v = [0]

                def dve_scale_pieces(n):
                    for p in range(sc_done[0], min(NPC, sc_done[0] + n)):
                        v.wait_ge(s_ld0.h[p % 2], 16 * (p // 2 + 1))
                        v.reduce_sum(redrs[:, p:p + 1], slabp[p % 2][:], AX.X,
                                     apply_absolute_value=True).then_inc(s_dq.h, 1)
                        s_dq.inc()
                    sc_done[0] = min(NPC, sc_done[0] + n)

                def dve_quant_pieces(n):
                    for p in range(qp_done_v[0], min(NPC, qp_done_v[0] + n)):
                        thrc = p_thr1 if p < 256 else p_thr2
                        nthrc = p_nthr1 if p < 256 else p_nthr2
                        v.wait_ge(s_ld0.h[p % 2], 16 * (256 + p // 2 + 1))
                        if p >= 2:
                            v.wait_ge(s_cw.h[p % 2], 16 * (p // 2))
                        v.tensor_scalar(recb[:], slabp[p % 2][:], thrc[:, 0:1], -1.0,
                                        ALU.is_ge, ALU.add)
                        v.scalar_tensor_tensor(qbq[p % 2][:], slabp[p % 2][:],
                                               nthrc[:, 0:1], recb[:],
                                               ALU.is_ge, ALU.add).then_inc(s_dq.h, 1)
                        s_dq.inc()
                    qp_done_v[0] = min(NPC, qp_done_v[0] + n)

                def dve_bulk(h, sb):
                    if sb >= NSB - 1:
                        return
                    if h <= 5:
                        dve_scale_pieces(13)
                    elif h >= 8:
                        dve_quant_pieces(10)

                def dve_inject(h):
                    if h == 6:
                        spacer(v)
                        v.reduce_sum(accr[:, 0:1], redrs[:, 0:256], AX.X)
                        spacer(v)
                        v.reduce_sum(accr[:, 1:2], redrs[:, 256:512], AX.X)
                        spacer(v)
                        v.tensor_copy(p_spc[:, 0:2], accr[:, 0:2]).then_inc(s_dq.h, 1)
                        s_dq.inc()
                    if h == 7:
                        v.wait_ge(s_pe0.h, 1)
                        v.tensor_scalar(p_thrs[0:1, 0:2], BK(0)[0:1, 0:2],
                                        0.7 / W_COUNT, None, ALU.mult)
                        spacer(v)
                        v.tensor_copy(p_spc[0:1, 0:2], p_thrs[0:1, 0:2]).then_inc(s_thr.h, 1)
                        s_thr.inc()
                    if h == 8:
                        v.wait_ge(s_pe0.h, 2)
                        v.tensor_copy(p_thr1[:], BK(0)[0:128, 0:1])
                        v.tensor_copy(p_thr2[:], BK(1)[0:128, 0:1])
                        spacer(v)
                        v.tensor_scalar(p_nthr1[:], p_thr1[:], -1.0, None, ALU.mult)
                        v.tensor_scalar(p_nthr2[:], p_thr2[:], -1.0, None, ALU.mult)
                        spacer(v)
                        v.tensor_copy(p_spc[:, 0:1], p_nthr1[:]).then_inc(s_thr.h, 1)
                        s_thr.inc()

                for g2 in range(8):
                    for hl in range(2):
                        h = 2 * g2 + hl
                        dve_inject(h)
                        for sb in range(NSB):
                            hs = hs_of(h, sb)
                            v.wait_ge(s_avd.h, cnt(hs, TC - 1))
                            if hs >= 2:
                                v.wait_ge(s_uw.h[hs % 2], 16 * (hs // 2))
                            v.reciprocal(recb[:], BK(4 + hs % 2))
                            v.tensor_tensor(ub[hs % 2][:], BK(2 + hs % 2),
                                            recb[:], ALU.mult)
                            v.tensor_scalar(ub[hs % 2][:], ub[hs % 2][:],
                                            p_bvT[:, h:h + 1], None,
                                            ALU.add).then_inc(s_nrm.h, 1)
                            s_nrm.inc()
                            dve_bulk(h, sb)
                dve_scale_pieces(NPC)
                dve_quant_pieces(NPC)

            @block.sync
            def _(sync):
                sync.wait_ge(s_uw.h[0], u_writes[0])
                sync.wait_ge(s_uw.h[1], u_writes[1])
                sync.nop().then_inc(s_b2.h, 1)
                s_b2.inc()
        b2v = s_b2.n
        s12.close()

        if last_phase < 3:
            @block.gpsimd
            def _(g):
                g.wait_ge(s_b2.h, b2v)
                if debug:
                    g.dma_start(out=dbg_xn.rearrange("(a p) j -> a p j", p=128),
                                in_=xnT_hbm.rearrange("c p s -> (c p) s").rearrange(
                                    "(a p) j -> a p j", p=128)).then_inc(s_owd.h, 16)
                    s_owd.inc(16)
                    g.dma_start(out=dbg_w1q.rearrange("a b p j -> (a b) p j"),
                                in_=w1q.rearrange("a b p j -> (a b) p j")).then_inc(s_owd.h, 16)
                    s_owd.inc(16)
                    g.dma_start(out=dbg_w2q.rearrange("a b p j -> (a b) p j"),
                                in_=w2q.rearrange("a b p j -> (a b) p j")).then_inc(s_owd.h, 16)
                    s_owd.inc(16)
                    g.dma_start(out=dbg_u.rearrange("h s p j -> (h s) p j"),
                                in_=u_sc.rearrange("h s p j -> (h s) p j")).then_inc(s_owd.h, 16)
                    s_owd.inc(16)
                    g.wait_ge(s_ow.h[0], s_ow.n[0])
                g.wait_ge(s_ow.h[1], s_ow.n[1])
                if s_owd.n:
                    g.wait_ge(s_owd.h, s_owd.n)
            return nc

        with ExitStack() as s3:
            e3 = s3.enter_context
            hT = e3(nc.sbuf_tensor("hT", [128, FC * SB], BF16))
            xnTb = [e3(nc.sbuf_tensor(f"xnTb{i}", [128, CC * SB], BF16)) for i in range(2)]
            w1b = [e3(nc.sbuf_tensor(f"w1b{i}", [128, 8 * SB], BF16)) for i in range(4)]
            w2b = [e3(nc.sbuf_tensor(f"w2b{i}", [128, 8 * SB], BF16)) for i in range(3)]
            wobf = [e3(nc.sbuf_tensor(f"wobf{i}", [128, 4 * SB], BF16)) for i in range(3)]
            ub3 = e3(nc.sbuf_tensor("ub3", [128, H * SB], BF16))
            xbb = [e3(nc.sbuf_tensor(f"xbb{i}", [128, SB], F32)) for i in range(4)]
            outb = [e3(nc.sbuf_tensor(f"outb{i}", [128, SB], F32)) for i in range(2)]
            bob2 = e3(nc.sbuf_tensor("bob2", [128, D], F32))
            bo2s = hT.bitcast(F32)[:, 0:D]

            @block.sync
            def _(sync):
                sync.wait_ge(s_b2.h, b2v)
                sync.dma_start(out=bob2[:], in_=bo_rep).then_inc(s_bb.h, 16)
                s_bb.inc(16)
                sync.dma_start(out=bo2s, in_=b2_rep).then_inc(s_bb.h, 16)
                s_bb.inc(16)
            bb_loads = s_bb.n

            @block.vector
            def _(v):
                v.wait_ge(s_bb.h, bb_loads)
                v.tensor_tensor(bob2[:], bob2[:], bo2s, ALU.add).then_inc(s_bb.h, 1)
                s_bb.inc()
            bb_done = s_bb.n

            @block.sync
            def _(sync):
                def w1_loads(k, fs_lo=0, fs_hi=NFS):
                    if k == 0 and fs_lo == 0:
                        for i in range(2):
                            sync.wait_ge(s_cw.h[i], cache_done[i])
                        sync.wait_ge(s_woc.h, wo_done)
                    for fs in range(fs_lo, fs_hi):
                        for half in range(2):
                            idx = k * 32 + fs * 2 + half
                            if idx >= 4:
                                sync.wait_ge(s_w1d.h, 2 * ((idx - 4) // 2 + 1))
                            sync.dma_start(
                                out=w1b[idx % 4].ap().rearrange("p (c j) -> p c j", j=SB),
                                in_=w1q[fs, half * 8:(half + 1) * 8, :, :].rearrange(
                                    "c p j -> p c j")).then_inc(s_w1l.h[idx % 4], 16)
                            s_w1l.inc(idx % 4)

                def xnt_load(k):
                    sync.dma_start(
                        out=xnTb[k % 2].ap().rearrange("p (c j) -> p c j", j=SB),
                        in_=xnT_hbm[:, :, k * SB:(k + 1) * SB].rearrange(
                            "c p j -> p c j")).then_inc(s_xnl.h[k % 2], 16)
                    s_xnl.inc(k % 2)

                def cb_streams(k):
                    if s3_mode == 'ffn1':
                        return
                    for cb in range(NCB):
                        if s3_mode != 'full':
                            for piece in range(8):
                                idx = (k * NCB + cb) * 8 + piece
                                if idx >= 3:
                                    sync.wait_ge(s_w2d.h, idx - 2)
                                sync.dma_start(
                                    out=w2b[idx % 3].ap().rearrange("p (c j) -> p c j", j=SB),
                                    in_=w2q[cb, piece * 8:(piece + 1) * 8, :, :].rearrange(
                                        "c p j -> p c j")).then_inc(s_w2l.h[idx % 3], 16)
                                s_w2l.inc(idx % 3)
                        if s3_mode == 'full':
                            for piece in range(4):
                                idx = (k * NCB + cb) * 4 + piece
                                if idx >= 3:
                                    sync.wait_ge(s_wod.h, idx - 2)
                                sync.dma_start(
                                    out=wobf[idx % 3].ap().rearrange("p (c j) -> p c j", j=SB),
                                    in_=woq[piece * 4:(piece + 1) * 4, :,
                                            cb * SB:(cb + 1) * SB].rearrange(
                                        "c p j -> p c j")).then_inc(s_wol.h[idx % 3], 16)
                                s_wol.inc(idx % 3)
                        for ss in range(4):
                            if s3_mode in ('ffn2noev', 'ffn2nx', 'ffn2dummy'):
                                continue
                            gidx = (k * NCB + cb) * 4 + ss
                            if gidx >= 4:
                                sync.wait_ge(s_oev.h, gidx - 3)
                            r0 = k * SB + ss * 128
                            sync.dma_start(out=xbb[ss][:],
                                           in_=x_in[r0:r0 + 128,
                                                    cb * SB:(cb + 1) * SB]).then_inc(s_xbl.h[ss], 16)
                            s_xbl.inc(ss)

                sync.wait_ge(s_xh.h, xh_done)
                xnt_load(0)
                w1_loads(0)
                for k in range(NSB):
                    if k + 1 < NSB:
                        sync.wait_ge(s_hm.h, k * FC)
                        xnt_load(k + 1)
                        w1_loads(k + 1, 0, 2)
                    cb_streams(k)
                    if k + 1 < NSB:
                        w1_loads(k + 1, 2, NFS)

            @block.tensor
            def _(t):
                t.wait_ge(s_bb.h, bb_done)
                for k in range(NSB):
                    t.wait_ge(s_xnl.h[k % 2], 16 * (k // 2 + 1))
                    if k >= 1 and s3_mode in ('ffn2', 'ffn2nx', 'ffn2nw', 'ffn2nog', 'ffn2dummy', 'full'):
                        t.wait_ge(s_oev.h, (4 * k - 1) * 4)
                    for fs in range(NFS):
                        i0 = k * 32 + fs * 2
                        t.wait_ge(s_w1l.h[i0 % 4], 16 * (i0 // 4 + 1))
                        t.wait_ge(s_w1l.h[(i0 + 1) % 4], 16 * ((i0 + 1) // 4 + 1))
                        for f4 in range(4):
                            fc = fs * 4 + f4
                            bank = BK(fc % 2) if fc < FC - 2 else BK(6 + fc % 2)
                            if fc == FC - 2 and k >= 1:
                                t.wait_ge(s_oev.h, 16 * k)
                            if fc >= 2:
                                t.wait_ge(s_gel.h, k * FC + fc - 1)
                            for cc in range(CC):
                                half = cc // 8
                                idx = k * 32 + fs * 2 + half
                                wsl = w1b[idx % 4]
                                ccl = cc % 8
                                mm = t.matmul(bank,
                                              wsl[:, ccl * SB + f4 * 128: ccl * SB + (f4 + 1) * 128],
                                              xnTb[k % 2][:, cc * SB:(cc + 1) * SB],
                                              start=(cc == 0), stop=(cc == CC - 1))
                            mm.then_inc(s_hm.h, 1)
                            s_hm.inc()
                        t.matmul(BK((fs * 4 + 3) % 2)[:, 0:128], p_zeros[:],
                                 p_zeros[:], start=False, stop=True,
                                 skip_group_check=True).then_inc(s_w1d.h, 2)
                        s_w1d.inc(2)
                    if s3_mode == 'ffn1':
                        continue
                    for cb in range(NCB):
                        banks = [BK((cb % 2) * 4 + ss) for ss in range(4)]
                        if k * NCB + cb >= 2 and s3_mode in ('ffn2', 'ffn2nx', 'ffn2nw', 'ffn2nog', 'ffn2dummy', 'full'):
                            t.wait_ge(s_oev.h, (k * NCB + cb - 1) * 4)
                        for piece in range(8):
                            idx = (k * NCB + cb) * 8 + piece
                            if cb == 0:
                                t.wait_ge(s_gel.h,
                                          k * FC + max((piece + 1) * 8, FC - 2))
                            elif cb == 1 and piece == 0:
                                t.wait_ge(s_gel.h, (k + 1) * FC)
                            t.wait_ge(s_w2l.h[idx % 3], 16 * (idx // 3 + 1))
                            for fcl in range(8):
                                fc = piece * 8 + fcl
                                for ss in range(4):
                                    mm = t.matmul(banks[ss],
                                                  hT[:, fc * SB + ss * 128: fc * SB + (ss + 1) * 128],
                                                  w2b[idx % 3][:, fcl * SB:(fcl + 1) * SB],
                                                  start=(fc == 0), stop=False)
                            mm.then_inc(s_w2d.h, 1)
                            s_w2d.inc()
                        if s3_mode == 'full':
                            if cb == 0:
                                t.wait_ge(s_ul.h, 16 * (k + 1))
                            for piece in range(4):
                                idx = (k * NCB + cb) * 4 + piece
                                t.wait_ge(s_wol.h[idx % 3], 16 * (idx // 3 + 1))
                                for dcl in range(4):
                                    dc = piece * 4 + dcl
                                    for ss in range(4):
                                        mm = t.matmul(banks[ss],
                                                      ub3[:, dc * SB + ss * 128: dc * SB + (ss + 1) * 128],
                                                      wobf[idx % 3][:, dcl * SB:(dcl + 1) * SB],
                                                      start=False, stop=(dc == CC - 1))
                                if piece == 3:
                                    mm.then_inc(s_om.h, 4)
                                    s_om.inc(4)
                                    t.matmul(banks[3][:, 0:128], p_zeros[:],
                                             p_zeros[:], start=False, stop=True,
                                             skip_group_check=True).then_inc(s_wod.h, 1)
                                    s_wod.inc()
                                else:
                                    mm.then_inc(s_wod.h, 1)
                                    s_wod.inc()
                        else:
                            for ss in range(4):
                                mm = t.matmul(banks[ss],
                                              hT[:, ss * 128:(ss + 1) * 128],
                                              w2b[((k * NCB + cb) * 8 + 7) % 3][:, 0:SB],
                                              start=False, stop=True)
                            mm.then_inc(s_om.h, 4)
                            s_om.inc(4)

            @block.scalar
            def _(s):
                s.wait_ge(s_b2.h, b2v)
                for k in range(NSB):
                    for fc in range(FC):
                        gbank = BK(fc % 2) if fc < FC - 2 else BK(6 + fc % 2)
                        s.wait_ge(s_hm.h, k * FC + fc + 1)
                        s.activation(hT[:, fc * SB:(fc + 1) * SB], gbank,
                                     AF.Gelu, bias=p_b1T[:, fc:fc + 1]).then_inc(s_gel.h, 1)
                        s_gel.inc()


            @block.vector
            def _(v):
                v.wait_ge(s_bb.h, bb_done)
                for k in range(NSB):
                    if s3_mode in ('ffn1', 'ffn2noev'):
                        continue
                    for cb in range(NCB):
                        for ss in range(4):
                            gidx = (k * NCB + cb) * 4 + ss
                            v.wait_ge(s_om.h, (k * NCB + cb) * 4 + 4)
                            if s3_mode not in ('ffn2nx', 'ffn2dummy'):
                                v.wait_ge(s_xbl.h[ss], 16 * (k * NCB + cb + 1))
                            if gidx >= 2 and s3_mode not in ('ffn2nog', 'ffn2dummy'):
                                v.wait_ge(s_ow.h[gidx % 2], 16 * (gidx // 2))
                            bank = BK((cb % 2) * 4 + ss)
                            if s3_mode == 'ffn2dummy':
                                v.tensor_copy(outb[gidx % 2][:],
                                              bob2[:, cb * SB:(cb + 1) * SB]).then_inc(s_oev.h, 1)
                                s_oev.inc()
                            elif s3_mode != 'ffn2nx':
                                v.tensor_tensor(outb[gidx % 2][:], bank, xbb[ss][:], ALU.add)
                                v.tensor_tensor(outb[gidx % 2][:], outb[gidx % 2][:],
                                                bob2[:, cb * SB:(cb + 1) * SB],
                                                ALU.add).then_inc(s_oev.h, 1)
                            else:
                                v.tensor_tensor(outb[gidx % 2][:], bank[:],
                                                bob2[:, cb * SB:(cb + 1) * SB],
                                                ALU.add).then_inc(s_oev.h, 1)
                            s_oev.inc()

            @block.gpsimd
            def _(g):
                g.wait_ge(s_b2.h, b2v)

                def out_writes(k, cb):
                    if s3_mode in ('ffn1', 'ffn2noev', 'ffn2nog', 'ffn2dummy'):
                        return
                    for ss in range(4):
                        gidx = (k * NCB + cb) * 4 + ss
                        g.wait_ge(s_oev.h, gidx + 1)
                        r0 = k * SB + ss * 128
                        _dst = out_scr if s3_mode == 'ffn2nw' else out_hbm
                        g.dma_start(out=_dst[r0:r0 + 128, cb * SB:(cb + 1) * SB],
                                    in_=outb[gidx % 2][:]).then_inc(s_ow.h[gidx % 2], 16)
                        s_ow.inc(gidx % 2)

                def w2_stream(k, cb):
                    for piece in range(8):
                        idx = (k * NCB + cb) * 8 + piece
                        if idx >= 3:
                            g.wait_ge(s_w2d.h, idx - 2)
                        g.dma_start(
                            out=w2b[idx % 3].ap().rearrange("p (c j) -> p c j", j=SB),
                            in_=w2q[cb, piece * 8:(piece + 1) * 8, :, :].rearrange(
                                "c p j -> p c j")).then_inc(s_w2l.h[idx % 3], 16)
                        s_w2l.inc(idx % 3)

                def u_load_g(k):
                    if k >= 1:
                        g.wait_ge(s_om.h, k * 16)
                    g.dma_start(out=ub3.ap().rearrange("p (h j) -> p h j", j=SB),
                                in_=u_sc[:, k, :, :].rearrange("h p j -> p h j")).then_inc(s_ul.h, 16)
                    s_ul.inc(16)

                if s3_mode == 'full':
                    for i in range(2):
                        g.wait_ge(s_cw.h[i], cache_done[i])
                    seq = [(k, cb) for k in range(NSB) for cb in range(NCB)]
                    u_load_g(0)
                    w2_stream(*seq[0])
                    for j in range(1, len(seq)):
                        if seq[j][1] == 0:
                            u_load_g(seq[j][0])
                        out_writes(*seq[j - 1])
                        w2_stream(*seq[j])
                    out_writes(*seq[-1])
                else:
                    for k in range(NSB):
                        if s3_mode == 'ffn1':
                            continue
                        for cb in range(NCB):
                            out_writes(k, cb)
                if s3_mode == 'ffn1':
                    g.wait_ge(s_gel.h, NSB * FC)
                if s3_mode == 'ffn2noev':
                    g.wait_ge(s_om.h, NSB * 16)
                if s3_mode in ('ffn2nog', 'ffn2dummy'):
                    g.wait_ge(s_oev.h, NSB * NCB * 4)
                if debug:
                    g.dma_start(out=dbg_ht, in_=hT[:]).then_inc(s_owd.h, 16)
                    s_owd.inc(16)
                    g.dma_start(out=dbg_xnt, in_=xnTb[(NSB - 1) % 2][:]).then_inc(s_owd.h, 16)
                    s_owd.inc(16)
                    g.dma_start(out=dbg_ub3, in_=ub3[:]).then_inc(s_owd.h, 16)
                    s_owd.inc(16)
                if debug:
                    g.dma_start(out=dbg_xn, in_=xn_sc).then_inc(s_owd.h, 16)
                    s_owd.inc(16)
                    g.dma_start(out=dbg_u.rearrange("h s p j -> (h s) p j"),
                                in_=u_sc.rearrange("h s p j -> (h s) p j")).then_inc(s_owd.h, 16)
                    s_owd.inc(16)
                    g.dma_start(out=dbg_w1q.rearrange("a b p j -> (a b) p j"),
                                in_=w1q.rearrange("a b p j -> (a b) p j")).then_inc(s_owd.h, 16)
                    s_owd.inc(16)
                    g.dma_start(out=dbg_w2q.rearrange("a b p j -> (a b) p j"),
                                in_=w2q.rearrange("a b p j -> (a b) p j")).then_inc(s_owd.h, 16)
                    s_owd.inc(16)
                g.wait_ge(s_ow.h[0], s_ow.n[0])
                g.wait_ge(s_ow.h[1], s_ow.n[1])
                if s_owd.n:
                    g.wait_ge(s_owd.h, s_owd.n)

    return nc


def kernel(L0_attn_x, L0_attn_Wq, L0_attn_bq, L0_attn_Wk, L0_attn_bk,
           L0_attn_Wv, L0_attn_bv, L0_attn_Wo, L0_attn_bo,
           L0_attn_spiral_weight, L0_attn_helix_rotation,
           L0_ffn_w1, L0_ffn_b1, L0_ffn_w2, L0_ffn_b2):
    f32 = np.float32
    x = np.ascontiguousarray(np.asarray(L0_attn_x, f32))
    wq = np.asarray(L0_attn_Wq, f32); wk = np.asarray(L0_attn_Wk, f32)
    wv = np.asarray(L0_attn_Wv, f32); wo = np.asarray(L0_attn_Wo, f32)
    w1 = np.asarray(L0_ffn_w1, f32); w2 = np.asarray(L0_ffn_w2, f32)

    shared = dict(
        wqT=np.ascontiguousarray(wq.T),
        wkT=np.ascontiguousarray(wk.T),
        wvT=np.ascontiguousarray(wv.T),
        woT=np.ascontiguousarray(wo.T),
        w1s=np.ascontiguousarray(w1.T.reshape(CC, 128, NFS, 512).transpose(2, 0, 1, 3)),
        w2s=np.ascontiguousarray(w2.T.reshape(FC, 128, NCB, 512).transpose(2, 0, 1, 3)),
        bqT=np.ascontiguousarray(np.asarray(L0_attn_bq, f32).reshape(H, 128).T),
        bkT=np.ascontiguousarray(np.asarray(L0_attn_bk, f32).reshape(H, 128).T),
        rotT=np.ascontiguousarray(np.asarray(L0_attn_helix_rotation, f32).reshape(H, 128).T),
        spT=np.ascontiguousarray(np.broadcast_to(np.asarray(L0_attn_spiral_weight, f32), (128, H))),
        b1T=np.ascontiguousarray(np.asarray(L0_ffn_b1, f32).reshape(FC, 128).T),
        bo_rep=np.ascontiguousarray(np.broadcast_to(np.asarray(L0_attn_bo, f32), (128, D))),
        b2_rep=np.ascontiguousarray(np.broadcast_to(np.asarray(L0_ffn_b2, f32), (128, D))),
        bvT=np.ascontiguousarray(np.asarray(L0_attn_bv, f32).reshape(H, 128).T),
    )
    in_maps = [dict(x=np.ascontiguousarray(x[b]), **shared) for b in range(8)]
    nc = build()
    res = run_bass_kernel_spmd(nc, in_maps, core_ids=list(range(8)))
    return np.stack([np.asarray(res.results[b]["out"], f32) for b in range(8)], axis=0)


if __name__ == "__main__":
    build()
    print("built ok")


# version 27 (final)
# speedup vs baseline: 1.1578x; 1.1222x over previous
import numpy as np
import ml_dtypes
from contextlib import ExitStack

import concourse.bass as bass
import concourse.mybir as mybir
from concourse.bass_utils import run_bass_kernel_spmd

F32 = mybir.dt.float32
BF16 = mybir.dt.bfloat16
AF = mybir.ActivationFunctionType
ALU = mybir.AluOpType
AX = mybir.AxisListType

S, D, H, DH, DFF = 4096, 2048, 16, 128, 8192
CC = D // 128
TC = S // 128
FC = DFF // 128
SB = 512
NSB = S // SB
NFS = 16
NCB = 4
INV_SQRT_DH = float(1.0 / np.sqrt(128.0))
LN_EPS = 1e-5
TWO_PI = float(2.0 * np.pi)
RND_MAGIC = 12582912.0
W_COUNT = float(DFF * D)


class Ctr:
    def __init__(self, stack, nc, name):
        self.h = stack.enter_context(nc.semaphore(name))
        self.n = 0

    def inc(self, k=1):
        self.n += k
        return self.n


class SlotCtr:
    def __init__(self, stack, nc, name, nslots):
        self.h = [stack.enter_context(nc.semaphore(f"{name}{i}")) for i in range(nslots)]
        self.n = [0] * nslots

    def inc(self, slot, k=16):
        self.n[slot] += k
        return self.n[slot]


def build(debug=False, last_phase=3, s3_mode='full'):
    nc = bass.Bass("TRN2", target_bir_lowering=False)

    x_in = nc.dram_tensor("x", [S, D], F32, kind="ExternalInput").ap()
    wqT = nc.dram_tensor("wqT", [D, D], F32, kind="ExternalInput").ap()
    wkT = nc.dram_tensor("wkT", [D, D], F32, kind="ExternalInput").ap()
    wvT = nc.dram_tensor("wvT", [D, D], F32, kind="ExternalInput").ap()
    woT = nc.dram_tensor("woT", [D, D], F32, kind="ExternalInput").ap()
    w1s = nc.dram_tensor("w1s", [NFS, CC, 128, 512], F32, kind="ExternalInput").ap()
    w2s = nc.dram_tensor("w2s", [NCB, FC, 128, 512], F32, kind="ExternalInput").ap()
    bqT_in = nc.dram_tensor("bqT", [128, H], F32, kind="ExternalInput").ap()
    bkT_in = nc.dram_tensor("bkT", [128, H], F32, kind="ExternalInput").ap()
    rotT_in = nc.dram_tensor("rotT", [128, H], F32, kind="ExternalInput").ap()
    spT_in = nc.dram_tensor("spT", [128, H], F32, kind="ExternalInput").ap()
    b1T_in = nc.dram_tensor("b1T", [128, FC], F32, kind="ExternalInput").ap()
    bo_rep = nc.dram_tensor("bo_rep", [128, D], F32, kind="ExternalInput").ap()
    b2_rep = nc.dram_tensor("b2_rep", [128, D], F32, kind="ExternalInput").ap()
    bvT_in = nc.dram_tensor("bvT", [128, H], F32, kind="ExternalInput").ap()

    out_hbm = nc.dram_tensor("out", [S, D], F32, kind="ExternalOutput").ap()

    xn_sc = nc.dram_tensor("xn_sc", [S, D], BF16, kind="Internal").ap()
    u_sc = nc.dram_tensor("u_sc", [H, NSB, 128, SB], BF16, kind="Internal").ap()
    w1q = nc.dram_tensor("w1q", [NFS, CC, 128, 512], BF16, kind="Internal").ap()
    w2q = nc.dram_tensor("w2q", [NCB, FC, 128, 512], BF16, kind="Internal").ap()
    woq = nc.dram_tensor("woq", [CC, 128, D], BF16, kind="Internal").ap()
    out_scr = nc.dram_tensor("out_scr", [S, D], F32, kind="Internal").ap()
    xnT_hbm = nc.dram_tensor("xnT_hbm", [CC, 128, S], BF16, kind="Internal").ap()

    if debug:
        dbg_ht = nc.dram_tensor("dbg_ht", [128, FC * SB], BF16, kind="ExternalOutput").ap()
        dbg_xnt = nc.dram_tensor("dbg_xnt", [128, CC * SB], BF16, kind="ExternalOutput").ap()
        dbg_ub3 = nc.dram_tensor("dbg_ub3", [128, H * SB], BF16, kind="ExternalOutput").ap()
        dbg_xn = nc.dram_tensor("dbg_xn", [S, D], BF16, kind="ExternalOutput").ap()
        dbg_u = nc.dram_tensor("dbg_u", [H, NSB, 128, SB], BF16, kind="ExternalOutput").ap()
        dbg_w1q = nc.dram_tensor("dbg_w1q", [NFS, CC, 128, 512], BF16, kind="ExternalOutput").ap()
        dbg_w2q = nc.dram_tensor("dbg_w2q", [NCB, FC, 128, 512], BF16, kind="ExternalOutput").ap()

    ones_bf = nc.inline_tensor(np.ones((128, 128), dtype=ml_dtypes.bfloat16), "ones_bf").ap()
    ones_r = nc.inline_tensor(np.ones((1, 128), dtype=np.float32), "ones_r").ap()
    ones_c = nc.inline_tensor(np.ones((128, 1), dtype=np.float32), "ones_c").ap()
    zeros_b = nc.inline_tensor(np.zeros((128, 128), dtype=ml_dtypes.bfloat16), "zeros_b").ap()
    ident_b = nc.inline_tensor(np.eye(128, dtype=ml_dtypes.bfloat16), "ident_b").ap()

    with ExitStack() as root:
        e = root.enter_context

        p_bqT = e(nc.sbuf_tensor("p_bqT", [128, H], F32))
        p_bkT = e(nc.sbuf_tensor("p_bkT", [128, H], F32))
        p_bvT = e(nc.sbuf_tensor("p_bvT", [128, H], F32))
        p_qsc = e(nc.sbuf_tensor("p_qsc", [128, H], F32))
        p_ksc = e(nc.sbuf_tensor("p_ksc", [128, H], F32))
        p_bqe = e(nc.sbuf_tensor("p_bqe", [128, H], F32))
        p_bke = e(nc.sbuf_tensor("p_bke", [128, H], F32))
        p_b1T = e(nc.sbuf_tensor("p_b1T", [128, FC], F32))
        p_ones = e(nc.sbuf_tensor("p_ones", [128, 128], BF16))
        p_onesr = e(nc.sbuf_tensor("p_onesr", [1, 128], F32))
        p_onesc = e(nc.sbuf_tensor("p_onesc", [128, 1], F32))
        p_zeros = e(nc.sbuf_tensor("p_zeros", [128, 128], BF16))
        p_identb = e(nc.sbuf_tensor("p_identb", [128, 128], BF16))
        p_thr1 = e(nc.sbuf_tensor("p_thr1", [128, 1], F32))
        p_thr2 = e(nc.sbuf_tensor("p_thr2", [128, 1], F32))
        p_nthr1 = e(nc.sbuf_tensor("p_nthr1", [128, 1], F32))
        p_nthr2 = e(nc.sbuf_tensor("p_nthr2", [128, 1], F32))
        p_t1 = e(nc.sbuf_tensor("p_t1", [128, H], F32))
        p_t2 = e(nc.sbuf_tensor("p_t2", [128, H], F32))
        p_t3 = e(nc.sbuf_tensor("p_t3", [128, H], F32))
        p_thrs = e(nc.sbuf_tensor("p_thrs", [1, 2], F32))
        p_eps = e(nc.sbuf_tensor("p_eps", [128, 1], F32))
        p_spc = e(nc.sbuf_tensor("p_spc", [128, 512], F32))

        def spacer(v):
            v.tensor_copy(p_spc[:, 0:512], p_spc[:, 0:512])

        pbA = e(nc.psum_tensor("pbA", [128, 1024], F32))
        pb_mid = [e(nc.psum_tensor(f"pbm{i}", [128, 512], F32)) for i in range(4)]
        pbB = e(nc.psum_tensor("pbB", [128, 1024], F32))

        def BK(i):
            if i == 0:
                return pbA[:, 0:512]
            if i == 1:
                return pbA[:, 512:1024]
            if i == 6:
                return pbB[:, 0:512]
            if i == 7:
                return pbB[:, 512:1024]
            return pb_mid[i - 2][:]

        s_p = Ctr(root, nc, "s_p")
        s_pc = Ctr(root, nc, "s_pc")
        s_pe0 = Ctr(root, nc, "s_pe0")
        s_ld0 = SlotCtr(root, nc, "s_ld0", 2)
        s_dq = Ctr(root, nc, "s_dq")
        s_sg = Ctr(root, nc, "s_sg")
        s_cw = SlotCtr(root, nc, "s_cw", 2)
        s_woc = Ctr(root, nc, "s_woc")
        s_x = SlotCtr(root, nc, "s_x", 4)
        s_ln = Ctr(root, nc, "s_ln")
        s_lna = Ctr(root, nc, "s_lna")
        s_xnw = SlotCtr(root, nc, "s_xnw", 2)
        s_b1 = Ctr(root, nc, "s_b1")
        s_thr = Ctr(root, nc, "s_thr")
        s_xt = Ctr(root, nc, "s_xt")
        s_tp = Ctr(root, nc, "s_tp")
        s_xnT = Ctr(root, nc, "s_xnT")
        s_wv = Ctr(root, nc, "s_wv")
        s_wqk = Ctr(root, nc, "s_wqk")
        s_pvm = Ctr(root, nc, "s_pvm")
        s_vev = Ctr(root, nc, "s_vev")
        s_qkm = Ctr(root, nc, "s_qkm")
        s_qke = Ctr(root, nc, "s_qke")
        s_stm = Ctr(root, nc, "s_stm")
        s_exp = Ctr(root, nc, "s_exp")
        s_avd = Ctr(root, nc, "s_avd")
        s_nrm = Ctr(root, nc, "s_nrm")
        s_uw = SlotCtr(root, nc, "s_uw", 2)
        s_b2 = Ctr(root, nc, "s_b2")
        s_bb = Ctr(root, nc, "s_bb")
        s_ul = Ctr(root, nc, "s_ul")
        s_xh = Ctr(root, nc, "s_xh")
        s_xnl = SlotCtr(root, nc, "s_xnl", 2)
        s_w1l = SlotCtr(root, nc, "s_w1l", 4)
        s_w1d = Ctr(root, nc, "s_w1d")
        s_w2l = SlotCtr(root, nc, "s_w2l", 3)
        s_w2d = Ctr(root, nc, "s_w2d")
        s_wol = SlotCtr(root, nc, "s_wol", 4)
        s_wod = Ctr(root, nc, "s_wod")
        s_xbl = SlotCtr(root, nc, "s_xbl", 4)
        s_hm = Ctr(root, nc, "s_hm")
        s_gel = Ctr(root, nc, "s_gel")
        s_om = Ctr(root, nc, "s_om")
        s_oev = Ctr(root, nc, "s_oev")
        s_ow = SlotCtr(root, nc, "s_ow", 2)
        s_owd = Ctr(root, nc, "s_owd")

        block = e(nc.Block())

        @block.sync
        def _(sync):
            for dst, src in [(p_bqT, bqT_in), (p_bkT, bkT_in), (p_bvT, bvT_in),
                             (p_t1, rotT_in),
                             (p_t2, spT_in), (p_b1T, b1T_in)]:
                sync.dma_start(out=dst[:], in_=src).then_inc(s_p.h, 16)
                s_p.inc(16)
            for dst, src in [(p_ones, ones_bf), (p_onesr, ones_r), (p_onesc, ones_c),
                             (p_zeros, zeros_b), (p_identb, ident_b)]:
                sync.dma_start(out=dst[:], in_=src).then_inc(s_p.h, 16)
                s_p.inc(16)
        p_loads = s_p.n

        @block.vector
        def _(v):
            v.wait_ge(s_p.h, p_loads)
            v.memset(p_eps[:], LN_EPS)
            v.tensor_scalar(p_t3[:], p_t1[:], 1.0 / TWO_PI, RND_MAGIC, ALU.mult, ALU.add)
            spacer(v)
            v.tensor_scalar(p_t3[:], p_t3[:], RND_MAGIC, None, ALU.subtract)
            spacer(v)
            v.scalar_tensor_tensor(p_ksc[:], p_t3[:], -TWO_PI, p_t1[:],
                                   ALU.mult, ALU.add)
            spacer(v)
            v.tensor_copy(p_spc[:, 0:16], p_ksc[:]).then_inc(s_pc.h, 1)
            s_pc.inc()
        rr_done = s_pc.n

        @block.scalar
        def _(s):
            s.wait_ge(s_pc.h, rr_done)
            s.activation(p_t3[:], p_ksc[:], AF.Sin, scale=0.5).then_inc(s_sg.h, 1)
            s_sg.inc()
            s.activation(p_t1[:], p_ksc[:], AF.Sin).then_inc(s_sg.h, 1)
            s_sg.inc()
        trig_acts = s_sg.n

        @block.vector
        def _(v):
            v.wait_ge(s_sg.h, trig_acts)
            v.tensor_tensor(p_qsc[:], p_t3[:], p_t3[:], ALU.mult)
            spacer(v)
            v.tensor_scalar(p_qsc[:], p_qsc[:], -2.0, 1.0, ALU.mult, ALU.add)
            spacer(v)
            v.tensor_tensor(p_qsc[:], p_qsc[:], p_t2[:], ALU.mult)
            spacer(v)
            v.tensor_scalar(p_qsc[:], p_qsc[:], INV_SQRT_DH, None, ALU.mult)
            spacer(v)
            v.tensor_copy(p_ksc[:], p_t1[:])
            spacer(v)
            v.tensor_tensor(p_bqe[:], p_bqT[:], p_qsc[:], ALU.mult)
            spacer(v)
            v.tensor_tensor(p_bke[:], p_bkT[:], p_ksc[:], ALU.mult)
            spacer(v)
            v.tensor_copy(p_spc[:, 0:16], p_qsc[:]).then_inc(s_pc.h, 1)
            s_pc.inc()
        trig_done = s_pc.n

        s12 = ExitStack()
        xnT = s12.enter_context(nc.sbuf_tensor("xnT", [128, CC * S], BF16))

        with ExitStack() as s01:
            e01 = s01.enter_context
            xti = [e01(nc.sbuf_tensor(f"xti{i}", [128, D], F32)) for i in range(4)]
            xnb = [e01(nc.sbuf_tensor(f"xnb{i}", [128, D], BF16)) for i in range(2)]
            lnscr = [e01(nc.sbuf_tensor(f"lnscr{i}", [128, D], BF16)) for i in range(2)]
            st_sums = e01(nc.sbuf_tensor("st_sums", [128, TC], F32))
            st_sumsq = e01(nc.sbuf_tensor("st_sumsq", [128, TC], F32))
            st_nmean = e01(nc.sbuf_tensor("st_nmean", [128, TC], F32))
            st_nm2 = e01(nc.sbuf_tensor("st_nm2", [128, TC], F32))
            st_msq = e01(nc.sbuf_tensor("st_msq", [128, TC], F32))
            st_var = e01(nc.sbuf_tensor("st_var", [128, TC], F32))
            st_sqv = e01(nc.sbuf_tensor("st_sqv", [128, TC], F32))
            st_rstd = e01(nc.sbuf_tensor("st_rstd", [128, TC], F32))

            def x_loads(eng, par):
                for i in range(par, TC, 2):
                    if i >= 4:
                        eng.wait_ge(s_ln.h, 2 * (i - 3))
                        eng.wait_ge(s_lna.h, i - 3)
                    eng.dma_start(out=xti[i % 4][:],
                                  in_=x_in[i * 128:(i + 1) * 128, :]).then_inc(s_x.h[i % 4], 16)
                    s_x.inc(i % 4)
                for i in range(par, TC, 2):
                    if i < 4:
                        eng.wait_ge(s_ln.h, 2 * (TC - 4 + i) + 2)
                        eng.wait_ge(s_lna.h, TC - 4 + i + 1)
                    else:
                        eng.wait_ge(s_ln.h, 66 + i - 3)
                    eng.dma_start(out=xti[i % 4][:],
                                  in_=x_in[i * 128:(i + 1) * 128, :]).then_inc(s_x.h[i % 4], 16)
                    s_x.inc(i % 4)

            @block.sync
            def _(sync):
                x_loads(sync, 0)

            @block.gpsimd
            def _(g):
                x_loads(g, 1)

            @block.scalar
            def _(s):
                for i in range(TC):
                    s.wait_ge(s_x.h[i % 4], 16 * (i // 4 + 1))
                    if i >= 2:
                        s.wait_ge(s_ln.h, 2 * (i - 1))
                    s.activation(lnscr[i % 2][:], xti[i % 4][:],
                                 AF.Square).then_inc(s_lna.h, 1)
                    s_lna.inc()
                s.wait_ge(s_ln.h, 65)
                s.activation(st_sqv[:], st_var[:], AF.Sqrt,
                             bias=p_eps[:, 0:1]).then_inc(s_lna.h, 1)
                s_lna.inc()
            sqv_done = s_lna.n

            @block.vector
            def _(v):
                for i in range(TC):
                    v.wait_ge(s_x.h[i % 4], 16 * (i // 4 + 1))
                    v.reduce_sum(st_sums[:, i:i + 1], xti[i % 4][:],
                                 AX.X).then_inc(s_ln.h, 1)
                    v.wait_ge(s_lna.h, i + 1)
                    v.reduce_sum(st_sumsq[:, i:i + 1], lnscr[i % 2][:],
                                 AX.X).then_inc(s_ln.h, 1)
                spacer(v)
                v.tensor_scalar(st_nmean[:], st_sums[:], -1.0 / D, None, ALU.mult)
                spacer(v)
                v.tensor_tensor(st_nm2[:], st_nmean[:], st_nmean[:], ALU.mult)
                v.tensor_scalar(st_msq[:], st_sumsq[:], 1.0 / D, None, ALU.mult)
                spacer(v)
                v.tensor_tensor(st_var[:], st_msq[:], st_nm2[:], ALU.subtract)
                spacer(v)
                v.tensor_copy(p_spc[:, 0:32], st_var[:]).then_inc(s_ln.h, 1)
                s_ln.inc(2 * TC + 1)
                v.wait_ge(s_lna.h, sqv_done)
                v.reciprocal(st_rstd[:], st_sqv[:])
                spacer(v)
                v.tensor_copy(p_spc[:, 0:32], st_rstd[:]).then_inc(s_ln.h, 1)
                s_ln.inc(1)
                for i in range(TC):
                    v.wait_ge(s_x.h[i % 4], 16 * (TC // 4 + i // 4 + 1))
                    if i >= 2:
                        v.wait_ge(s_tp.h, (i - 1) * CC)
                    v.tensor_scalar(xnb[i % 2][:], xti[i % 4][:],
                                    st_nmean[:, i:i + 1], st_rstd[:, i:i + 1],
                                    ALU.add, ALU.mult).then_inc(s_ln.h, 1)
                    s_ln.inc(1)

            @block.tensor
            def _(t):
                t.wait_ge(s_p.h, p_loads)
                pAb = pbA.bitcast(BF16)
                for i in range(TC):
                    t.wait_ge(s_ln.h, 66 + i + 1)
                    for c in range(CC):
                        j = i * CC + c
                        if j >= 2:
                            t.wait_ge(s_xt.h, j - 1)
                        t.transpose(pAb[:, (j % 2) * 1024:(j % 2) * 1024 + 128],
                                    xnb[i % 2][:, c * 128:(c + 1) * 128],
                                    p_identb[:]).then_inc(s_tp.h, 1)
                        s_tp.inc()

            @block.scalar
            def _(s):
                for i in range(TC):
                    for c in range(CC):
                        j = i * CC + c
                        s.wait_ge(s_tp.h, j + 1)
                        s.copy(xnT[:, c * S + i * 128: c * S + (i + 1) * 128],
                               pbA.bitcast(BF16)[:, (j % 2) * 1024:(j % 2) * 1024 + 128]
                               ).then_inc(s_xt.h, 1)
                        s_xt.inc()
            xt_done = s_xt.n

        @block.sync
        def _(sync):
            sync.wait_ge(s_xt.h, xt_done)
            sync.nop().then_inc(s_b1.h, 1)
            s_b1.inc()
        b1v = s_b1.n

        def hs_of(h, sb):
            return h * NSB + sb

        def cnt(hs, tc):
            return hs * TC + tc + 1

        VB = (0, 1, 6, 7)

        def pcnt(hs):
            return (hs + 1) * (TC // 2)

        if last_phase < 2:
            @block.gpsimd
            def _(g):
                g.wait_ge(s_xt.h, xt_done)
            return nc

        NPC = 512

        with ExitStack() as s2:
            e2 = s2.enter_context
            vg = e2(nc.sbuf_tensor("vg", [128, TC * 256], BF16))
            wvb = e2(nc.sbuf_tensor("wvb", [128, CC * 256], BF16))
            wqb = e2(nc.sbuf_tensor("wqb", [128, CC * 128], BF16))
            wkb = e2(nc.sbuf_tensor("wkb", [128, CC * 128], BF16))
            qt = e2(nc.sbuf_tensor("qt", [128, S], BF16))
            kt = e2(nc.sbuf_tensor("kt", [128, S], BF16))
            eb = [e2(nc.sbuf_tensor(f"eb{i}", [128, 2 * SB], BF16)) for i in range(2)]
            recb = e2(nc.sbuf_tensor("recb", [128, SB], F32))
            ub = [e2(nc.sbuf_tensor(f"ub{i}", [128, SB], BF16)) for i in range(2)]
            slabp = [e2(nc.sbuf_tensor(f"slabp{i}", [128, 512], F32)) for i in range(2)]
            qbq = [e2(nc.sbuf_tensor(f"qbq{i}", [128, 512], BF16)) for i in range(2)]
            redrs = e2(nc.sbuf_tensor("redrs", [128, NPC], F32))
            accr = e2(nc.sbuf_tensor("accr", [128, 2], F32))
            wob = e2(nc.sbuf_tensor("wob", [128, D], BF16))

            w1f = w1s.rearrange("a b p j -> (a b) p j")
            w2f = w2s.rearrange("a b p j -> (a b) p j")
            w1qf = w1q.rearrange("a b p j -> (a b) p j")
            w2qf = w2q.rearrange("a b p j -> (a b) p j")

            def psrc(p):
                return w1f[p] if p < 256 else w2f[p - 256]

            def pdst(p):
                return w1qf[p] if p < 256 else w2qf[p - 256]

            SC_AT = {0: 86, 1: 86, 2: 86, 3: 86, 4: 86, 5: 82}
            QP_AT = {h: 64 for h in range(8, 16)}

            @block.sync
            def _(sync):
                sync.wait_ge(s_b1.h, b1v)
                sync.nop().then_inc(s_xnT.h, 16)
                s_xnT.inc(16)
            xnT_done = s_xnT.n

            @block.sync
            def _(sync):
                sync.wait_ge(s_xnT.h, xnT_done)
                for c in range(CC):
                    sync.dma_start(out=xnT_hbm[c, :, :],
                                   in_=xnT[:, c * S:(c + 1) * S]).then_inc(s_xh.h, 16)
                    s_xh.inc(16)
            xh_done = s_xh.n

            @block.sync
            def _(sync):
                for p in range(NPC):
                    if p >= 2:
                        sync.wait_ge(s_dq.h, p - 1)
                    sync.dma_start(out=slabp[p % 2][:],
                                   in_=psrc(p)).then_inc(s_ld0.h[p % 2], 16)
                    s_ld0.inc(p % 2)
                for p in range(NPC + 2):
                    if 2 <= p < NPC + 2:
                        sync.wait_ge(s_dq.h, 512 + p)
                    if p < NPC:
                        if p < 2:
                            sync.wait_ge(s_dq.h, 511 + p)
                        sync.dma_start(out=slabp[p % 2][:],
                                       in_=psrc(p)).then_inc(s_ld0.h[p % 2], 16)
                        s_ld0.inc(p % 2)
                    if p >= 2:
                        sync.dma_start(out=pdst(p - 2),
                                       in_=qbq[(p - 2) % 2][:]).then_inc(s_cw.h[(p - 2) % 2], 16)
                        s_cw.inc((p - 2) % 2)

            @block.gpsimd
            def _(g):
                g.wait_ge(s_b1.h, b1v)
                g.nop().then_inc(s_wv.h, 16)
                s_wv.inc(16)

                def u_writes_for_head(hh):
                    for sb in range(NSB):
                        hs = hs_of(hh, sb)
                        g.wait_ge(s_nrm.h, hs + 1)
                        g.dma_start(out=u_sc[hh, sb, :, :],
                                    in_=ub[hs % 2][:]).then_inc(s_uw.h[hs % 2], 16)
                        s_uw.inc(hs % 2)

                def g_inject(h):
                    if 1 <= h <= 4:
                        for dc in range(4 * (h - 1), 4 * h):
                            if dc >= 1:
                                g.wait_ge(s_woc.h, 32 * dc)
                            g.dma_start(out=wob[:],
                                        in_=woT[dc * 128:(dc + 1) * 128, :]).then_inc(s_woc.h, 16)
                            s_woc.inc(16)
                            g.wait_ge(s_woc.h, s_woc.n)
                            g.dma_start(out=woq[dc, :, :], in_=wob[:]).then_inc(s_woc.h, 16)
                            s_woc.inc(16)

                for h in range(H):
                    if h % 2 == 0:
                        g2 = h // 2
                        if g2 >= 1:
                            g.wait_ge(s_pvm.h, g2 * (TC // 2))
                        src = wvT.rearrange("(cc p) (g j) -> p cc g j", p=128, j=256)[:, :, g2, :]
                        g.dma_start(out=wvb.ap().rearrange("p (c j) -> p c j", j=256),
                                    in_=src).then_inc(s_wv.h, 16)
                        s_wv.inc(16)
                    if h >= 1:
                        g.wait_ge(s_qkm.h, h * 16)
                    srcq = wqT.rearrange("(cc p) (h j) -> p cc h j", p=128, j=128)[:, :, h, :]
                    srck = wkT.rearrange("(cc p) (h j) -> p cc h j", p=128, j=128)[:, :, h, :]
                    g.dma_start(out=wqb.ap().rearrange("p (c j) -> p c j", j=128),
                                in_=srcq).then_inc(s_wqk.h, 16)
                    s_wqk.inc(16)
                    g.dma_start(out=wkb.ap().rearrange("p (c j) -> p c j", j=128),
                                in_=srck).then_inc(s_wqk.h, 16)
                    s_wqk.inc(16)
                    if h >= 1:
                        u_writes_for_head(h - 1)
                    g_inject(h)
                u_writes_for_head(H - 1)
            u_writes = list(s_uw.n)
            wo_done = s_woc.n
            cache_done = list(s_cw.n)

            @block.tensor
            def _(t):
                t.wait_ge(s_xnT.h, xnT_done)

                def pe_inject(h):
                    if h == 7:
                        t.wait_ge(s_exp.h, pcnt(hs_of(6, NSB - 1)))
                        t.wait_ge(s_vev.h, 4 * (TC // 2))
                        t.wait_ge(s_dq.h, 513)
                        t.matmul(BK(0)[0:1, 0:2], p_onesc[:, 0:1], accr[:, 0:2],
                                 start=True, stop=True)
                        t.matmul(BK(0)[0:1, 0:128], p_zeros[:, 0:1], p_zeros[:],
                                 start=False, stop=True,
                                 skip_group_check=True).then_inc(s_pe0.h, 1)
                        s_pe0.inc()
                    if h == 8:
                        t.wait_ge(s_exp.h, pcnt(hs_of(7, NSB - 1)))
                        t.wait_ge(s_vev.h, 5 * (TC // 2))
                        t.wait_ge(s_thr.h, 1)
                        t.matmul(BK(0)[0:128, 0:1], p_onesr[0:1, :], p_thrs[0:1, 0:1],
                                 start=True, stop=True)
                        t.matmul(BK(1)[0:128, 0:1], p_onesr[0:1, :], p_thrs[0:1, 1:2],
                                 start=True, stop=True)
                        t.matmul(BK(1)[0:1, 0:128], p_zeros[:, 0:1], p_zeros[:],
                                 start=False, stop=True,
                                 skip_group_check=True).then_inc(s_pe0.h, 1)
                        s_pe0.inc()

                for g2 in range(8):
                    t.wait_ge(s_wv.h, 16 * (g2 + 2))
                    if g2 >= 1:
                        t.wait_ge(s_exp.h, pcnt(hs_of(2 * g2 - 1, NSB - 1)))
                    for tc in range(TC):
                        vp = tc // 2
                        half = tc % 2
                        gp = g2 * (TC // 2) + vp
                        bank = BK(VB[vp % 4])
                        if half == 0 and gp >= 4:
                            t.wait_ge(s_vev.h, gp - 3)
                        for cc in range(CC):
                            mm = t.matmul(bank[:, half * 256:(half + 1) * 256],
                                          xnT[:, cc * S + tc * 128: cc * S + (tc + 1) * 128],
                                          wvb[:, cc * 256:(cc + 1) * 256],
                                          start=(cc == 0), stop=(cc == CC - 1))
                        if half == 1:
                            mm.then_inc(s_pvm.h, 1)
                            s_pvm.inc()
                    for hl in range(2):
                        h = 2 * g2 + hl
                        pe_inject(h)
                        t.wait_ge(s_wqk.h, 32 * (h + 1))
                        if h >= 1:
                            t.wait_ge(s_exp.h, pcnt(hs_of(h - 1, NSB - 1)))
                        if hl == 0:
                            t.wait_ge(s_vev.h, (g2 + 1) * (TC // 2))
                        for qk in range(2):
                            wb = wqb if qk == 0 else wkb
                            for sb in range(NSB):
                                grp = h * 16 + qk * 8 + sb
                                bank = BK(6 + grp % 2)
                                if grp >= 2:
                                    t.wait_ge(s_qke.h, grp - 1)
                                for cc in range(CC):
                                    mm = t.matmul(bank,
                                                  wb[:, cc * 128:(cc + 1) * 128],
                                                  xnT[:, cc * S + sb * SB: cc * S + (sb + 1) * SB],
                                                  start=(cc == 0), stop=(cc == CC - 1))
                                mm.then_inc(s_qkm.h, 1)
                                s_qkm.inc()
                        if hl == 0:
                            t.wait_ge(s_vev.h, (g2 + 1) * (TC // 2))
                        if h == 7:
                            t.wait_ge(s_thr.h, 1)
                        if h == 8:
                            t.wait_ge(s_thr.h, 2)
                        t.wait_ge(s_qke.h, h * 16 + 16)
                        for sb in range(NSB):
                            hs = hs_of(h, sb)
                            ub_bank = BK(2 + hs % 2)
                            den_bank = BK(4 + hs % 2)
                            if hs >= 2:
                                t.wait_ge(s_nrm.h, hs - 1)

                            def st_pair(p, sbq=None):
                                sq = sb if sbq is None else sbq
                                pbank = [pbA, pbB][p % 2]
                                for half in range(2):
                                    tc = 2 * p + half
                                    t.matmul(pbank[:, half * 512:(half + 1) * 512],
                                             kt[:, tc * 128:(tc + 1) * 128],
                                             qt[:, sq * SB:(sq + 1) * SB],
                                             start=True, stop=True).then_inc(s_stm.h, 1)
                                    s_stm.inc()

                            NP = TC // 2
                            if sb == 0:
                                if hs >= 1:
                                    t.wait_ge(s_exp.h, hs * NP - 1)
                                st_pair(0)
                                if hs >= 1:
                                    t.wait_ge(s_exp.h, hs * NP)
                                st_pair(1)
                            for p in range(NP):
                                t.wait_ge(s_exp.h, hs * NP + p + 1)
                                if p >= NP - 2 and sb < NSB - 1:
                                    st_pair(p - (NP - 2), sb + 1)
                                for half in range(2):
                                    tc = 2 * p + half
                                    t.matmul(ub_bank,
                                             vg[:, tc * 256 + hl * 128: tc * 256 + (hl + 1) * 128],
                                             eb[p % 2][:, half * SB:(half + 1) * SB],
                                             start=(tc == 0), stop=(tc == TC - 1))
                                    t.matmul(den_bank,
                                             p_ones[:],
                                             eb[p % 2][:, half * SB:(half + 1) * SB],
                                             start=(tc == 0), stop=(tc == TC - 1)).then_inc(s_avd.h, 1)
                                    s_avd.inc()
                                if p + 2 < NP:
                                    st_pair(p + 2)
                t.matmul(BK(0)[:, 0:128], p_zeros[:], p_zeros[:],
                         start=False, stop=True,
                         skip_group_check=True).then_inc(s_b2.h, 1)
                s_b2.inc()

            @block.scalar
            def _(s):
                s.wait_ge(s_pc.h, trig_done)
                for h in range(H):
                    if h % 2 == 0:
                        g2 = h // 2
                        for vp in range(TC // 2):
                            s.wait_ge(s_pvm.h, g2 * (TC // 2) + vp + 1)
                            s.copy(vg[:, vp * 512:(vp + 1) * 512],
                                   BK(VB[vp % 4])).then_inc(s_vev.h, 1)
                            s_vev.inc()
                    for qk in range(2):
                        dstt = qt if qk == 0 else kt
                        bcol = p_bqe if qk == 0 else p_bke
                        scol = p_qsc if qk == 0 else p_ksc
                        for sb in range(NSB):
                            grp = h * 16 + qk * 8 + sb
                            s.wait_ge(s_qkm.h, grp + 1)
                            s.activation(dstt[:, sb * SB:(sb + 1) * SB],
                                         BK(6 + grp % 2), AF.Identity,
                                         bias=bcol[:, h:h + 1],
                                         scale=scol[:, h:h + 1]).then_inc(s_qke.h, 1)
                            s_qke.inc()
                    for sb in range(NSB):
                        hs = hs_of(h, sb)
                        for p in range(TC // 2):
                            s.wait_ge(s_stm.h, cnt(hs, 2 * p + 1))
                            if p >= 2:
                                s.wait_ge(s_avd.h, cnt(hs, 2 * p - 3))
                            elif hs >= 1:
                                s.wait_ge(s_avd.h, cnt(hs - 1, TC - 3 + 2 * p))
                            s.activation(eb[p % 2][:], [pbA, pbB][p % 2][:],
                                         AF.Exp).then_inc(s_exp.h, 1)
                            s_exp.inc()

            @block.vector
            def _(v):
                v.wait_ge(s_xnT.h, xnT_done)
                sc_done = [0]
                qp_done_v = [0]

                def dve_scale_pieces(n):
                    for p in range(sc_done[0], min(NPC, sc_done[0] + n)):
                        v.wait_ge(s_ld0.h[p % 2], 16 * (p // 2 + 1))
                        v.reduce_sum(redrs[:, p:p + 1], slabp[p % 2][:], AX.X,
                                     apply_absolute_value=True).then_inc(s_dq.h, 1)
                        s_dq.inc()
                    sc_done[0] = min(NPC, sc_done[0] + n)

                def dve_quant_pieces(n):
                    for p in range(qp_done_v[0], min(NPC, qp_done_v[0] + n)):
                        thrc = p_thr1 if p < 256 else p_thr2
                        nthrc = p_nthr1 if p < 256 else p_nthr2
                        v.wait_ge(s_ld0.h[p % 2], 16 * (256 + p // 2 + 1))
                        if p >= 2:
                            v.wait_ge(s_cw.h[p % 2], 16 * (p // 2))
                        v.tensor_scalar(recb[:], slabp[p % 2][:], thrc[:, 0:1], -1.0,
                                        ALU.is_ge, ALU.add)
                        v.scalar_tensor_tensor(qbq[p % 2][:], slabp[p % 2][:],
                                               nthrc[:, 0:1], recb[:],
                                               ALU.is_ge, ALU.add).then_inc(s_dq.h, 1)
                        s_dq.inc()
                    qp_done_v[0] = min(NPC, qp_done_v[0] + n)

                def dve_bulk(h, sb):
                    if sb >= NSB - 1:
                        return
                    if h <= 5:
                        dve_scale_pieces(13)
                    elif h >= 8:
                        dve_quant_pieces(10)

                def dve_inject(h):
                    if h == 6:
                        spacer(v)
                        v.reduce_sum(accr[:, 0:1], redrs[:, 0:256], AX.X)
                        spacer(v)
                        v.reduce_sum(accr[:, 1:2], redrs[:, 256:512], AX.X)
                        spacer(v)
                        v.tensor_copy(p_spc[:, 0:2], accr[:, 0:2]).then_inc(s_dq.h, 1)
                        s_dq.inc()
                    if h == 7:
                        v.wait_ge(s_pe0.h, 1)
                        v.tensor_scalar(p_thrs[0:1, 0:2], BK(0)[0:1, 0:2],
                                        0.7 / W_COUNT, None, ALU.mult)
                        spacer(v)
                        v.tensor_copy(p_spc[0:1, 0:2], p_thrs[0:1, 0:2]).then_inc(s_thr.h, 1)
                        s_thr.inc()
                    if h == 8:
                        v.wait_ge(s_pe0.h, 2)
                        v.tensor_copy(p_thr1[:], BK(0)[0:128, 0:1])
                        v.tensor_copy(p_thr2[:], BK(1)[0:128, 0:1])
                        spacer(v)
                        v.tensor_scalar(p_nthr1[:], p_thr1[:], -1.0, None, ALU.mult)
                        v.tensor_scalar(p_nthr2[:], p_thr2[:], -1.0, None, ALU.mult)
                        spacer(v)
                        v.tensor_copy(p_spc[:, 0:1], p_nthr1[:]).then_inc(s_thr.h, 1)
                        s_thr.inc()

                for g2 in range(8):
                    for hl in range(2):
                        h = 2 * g2 + hl
                        dve_inject(h)
                        for sb in range(NSB):
                            hs = hs_of(h, sb)
                            v.wait_ge(s_avd.h, cnt(hs, TC - 1))
                            if hs >= 2:
                                v.wait_ge(s_uw.h[hs % 2], 16 * (hs // 2))
                            v.reciprocal(recb[:], BK(4 + hs % 2))
                            v.tensor_tensor(ub[hs % 2][:], BK(2 + hs % 2),
                                            recb[:], ALU.mult)
                            v.tensor_scalar(ub[hs % 2][:], ub[hs % 2][:],
                                            p_bvT[:, h:h + 1], None,
                                            ALU.add).then_inc(s_nrm.h, 1)
                            s_nrm.inc()
                            dve_bulk(h, sb)
                dve_scale_pieces(NPC)
                dve_quant_pieces(NPC)

            @block.sync
            def _(sync):
                sync.wait_ge(s_uw.h[0], u_writes[0])
                sync.wait_ge(s_uw.h[1], u_writes[1])
                sync.nop().then_inc(s_b2.h, 1)
                s_b2.inc()
        b2v = s_b2.n
        s12.close()

        if last_phase < 3:
            @block.gpsimd
            def _(g):
                g.wait_ge(s_b2.h, b2v)
                if debug:
                    g.dma_start(out=dbg_xn.rearrange("(a p) j -> a p j", p=128),
                                in_=xnT_hbm.rearrange("c p s -> (c p) s").rearrange(
                                    "(a p) j -> a p j", p=128)).then_inc(s_owd.h, 16)
                    s_owd.inc(16)
                    g.dma_start(out=dbg_w1q.rearrange("a b p j -> (a b) p j"),
                                in_=w1q.rearrange("a b p j -> (a b) p j")).then_inc(s_owd.h, 16)
                    s_owd.inc(16)
                    g.dma_start(out=dbg_w2q.rearrange("a b p j -> (a b) p j"),
                                in_=w2q.rearrange("a b p j -> (a b) p j")).then_inc(s_owd.h, 16)
                    s_owd.inc(16)
                    g.dma_start(out=dbg_u.rearrange("h s p j -> (h s) p j"),
                                in_=u_sc.rearrange("h s p j -> (h s) p j")).then_inc(s_owd.h, 16)
                    s_owd.inc(16)
                    g.wait_ge(s_ow.h[0], s_ow.n[0])
                g.wait_ge(s_ow.h[1], s_ow.n[1])
                if s_owd.n:
                    g.wait_ge(s_owd.h, s_owd.n)
            return nc

        with ExitStack() as s3:
            e3 = s3.enter_context
            hT = e3(nc.sbuf_tensor("hT", [128, FC * SB], BF16))
            xnTb = [e3(nc.sbuf_tensor(f"xnTb{i}", [128, CC * SB], BF16)) for i in range(2)]
            w1b = [e3(nc.sbuf_tensor(f"w1b{i}", [128, 8 * SB], BF16)) for i in range(4)]
            w2b = [e3(nc.sbuf_tensor(f"w2b{i}", [128, 8 * SB], BF16)) for i in range(3)]
            wobf = [e3(nc.sbuf_tensor(f"wobf{i}", [128, 4 * SB], BF16)) for i in range(3)]
            ub3 = e3(nc.sbuf_tensor("ub3", [128, H * SB], BF16))
            xbb = [e3(nc.sbuf_tensor(f"xbb{i}", [128, SB], F32)) for i in range(4)]
            outb = [e3(nc.sbuf_tensor(f"outb{i}", [128, SB], F32)) for i in range(2)]
            bob2 = e3(nc.sbuf_tensor("bob2", [128, D], F32))
            bo2s = hT.bitcast(F32)[:, 0:D]

            @block.sync
            def _(sync):
                sync.wait_ge(s_b2.h, b2v)
                sync.dma_start(out=bob2[:], in_=bo_rep).then_inc(s_bb.h, 16)
                s_bb.inc(16)
                sync.dma_start(out=bo2s, in_=b2_rep).then_inc(s_bb.h, 16)
                s_bb.inc(16)
            bb_loads = s_bb.n

            @block.vector
            def _(v):
                v.wait_ge(s_bb.h, bb_loads)
                v.tensor_tensor(bob2[:], bob2[:], bo2s, ALU.add).then_inc(s_bb.h, 1)
                s_bb.inc()
            bb_done = s_bb.n

            @block.sync
            def _(sync):
                def w1_loads(k, fs_lo=0, fs_hi=NFS):
                    if k == 0 and fs_lo == 0:
                        for i in range(2):
                            sync.wait_ge(s_cw.h[i], cache_done[i])
                        sync.wait_ge(s_woc.h, wo_done)
                    for fs in range(fs_lo, fs_hi):
                        for half in range(2):
                            idx = k * 32 + fs * 2 + half
                            if idx >= 4:
                                sync.wait_ge(s_w1d.h, 2 * ((idx - 4) // 2 + 1))
                            sync.dma_start(
                                out=w1b[idx % 4].ap().rearrange("p (c j) -> p c j", j=SB),
                                in_=w1q[fs, half * 8:(half + 1) * 8, :, :].rearrange(
                                    "c p j -> p c j")).then_inc(s_w1l.h[idx % 4], 16)
                            s_w1l.inc(idx % 4)

                def xnt_load(k):
                    sync.dma_start(
                        out=xnTb[k % 2].ap().rearrange("p (c j) -> p c j", j=SB),
                        in_=xnT_hbm[:, :, k * SB:(k + 1) * SB].rearrange(
                            "c p j -> p c j")).then_inc(s_xnl.h[k % 2], 16)
                    s_xnl.inc(k % 2)

                def cb_streams(k):
                    if s3_mode == 'ffn1':
                        return
                    for cb in range(NCB):
                        if s3_mode != 'full':
                            for piece in range(8):
                                idx = (k * NCB + cb) * 8 + piece
                                if idx >= 3:
                                    sync.wait_ge(s_w2d.h, idx - 2)
                                sync.dma_start(
                                    out=w2b[idx % 3].ap().rearrange("p (c j) -> p c j", j=SB),
                                    in_=w2q[cb, piece * 8:(piece + 1) * 8, :, :].rearrange(
                                        "c p j -> p c j")).then_inc(s_w2l.h[idx % 3], 16)
                                s_w2l.inc(idx % 3)
                        if s3_mode == 'full':
                            for piece in range(4):
                                idx = (k * NCB + cb) * 4 + piece
                                if idx >= 3:
                                    sync.wait_ge(s_wod.h, idx - 2)
                                sync.dma_start(
                                    out=wobf[idx % 3].ap().rearrange("p (c j) -> p c j", j=SB),
                                    in_=woq[piece * 4:(piece + 1) * 4, :,
                                            cb * SB:(cb + 1) * SB].rearrange(
                                        "c p j -> p c j")).then_inc(s_wol.h[idx % 3], 16)
                                s_wol.inc(idx % 3)
                        for ss in range(4):
                            if s3_mode in ('ffn2noev', 'ffn2nx', 'ffn2dummy'):
                                continue
                            gidx = (k * NCB + cb) * 4 + ss
                            if gidx >= 4:
                                sync.wait_ge(s_oev.h, gidx - 3)
                            r0 = k * SB + ss * 128
                            sync.dma_start(out=xbb[ss][:],
                                           in_=x_in[r0:r0 + 128,
                                                    cb * SB:(cb + 1) * SB]).then_inc(s_xbl.h[ss], 16)
                            s_xbl.inc(ss)

                sync.wait_ge(s_xh.h, xh_done)
                xnt_load(0)
                w1_loads(0)
                for k in range(NSB):
                    if k + 1 < NSB:
                        sync.wait_ge(s_hm.h, k * FC)
                        xnt_load(k + 1)
                        w1_loads(k + 1, 0, 2)
                    cb_streams(k)
                    if k + 1 < NSB:
                        w1_loads(k + 1, 2, NFS)

            @block.tensor
            def _(t):
                t.wait_ge(s_bb.h, bb_done)
                for k in range(NSB):
                    t.wait_ge(s_xnl.h[k % 2], 16 * (k // 2 + 1))
                    if k >= 1 and s3_mode in ('ffn2', 'ffn2nx', 'ffn2nw', 'ffn2nog', 'ffn2dummy', 'full'):
                        t.wait_ge(s_oev.h, (4 * k - 1) * 4)
                    for fs in range(NFS):
                        i0 = k * 32 + fs * 2
                        t.wait_ge(s_w1l.h[i0 % 4], 16 * (i0 // 4 + 1))
                        t.wait_ge(s_w1l.h[(i0 + 1) % 4], 16 * ((i0 + 1) // 4 + 1))
                        for f4 in range(4):
                            fc = fs * 4 + f4
                            bank = BK(fc % 2)
                            if fc >= 2:
                                t.wait_ge(s_gel.h, k * FC + fc - 1)
                            for cc in range(CC):
                                half = cc // 8
                                idx = k * 32 + fs * 2 + half
                                wsl = w1b[idx % 4]
                                ccl = cc % 8
                                mm = t.matmul(bank,
                                              wsl[:, ccl * SB + f4 * 128: ccl * SB + (f4 + 1) * 128],
                                              xnTb[k % 2][:, cc * SB:(cc + 1) * SB],
                                              start=(cc == 0), stop=(cc == CC - 1))
                            mm.then_inc(s_hm.h, 1)
                            s_hm.inc()
                        t.matmul(BK((fs * 4 + 3) % 2)[:, 0:128], p_zeros[:],
                                 p_zeros[:], start=False, stop=True,
                                 skip_group_check=True).then_inc(s_w1d.h, 2)
                        s_w1d.inc(2)
                    if s3_mode == 'ffn1':
                        continue
                    t.wait_ge(s_gel.h, (k + 1) * FC)
                    for cb in range(NCB):
                        banks = [BK((cb % 2) * 4 + ss) for ss in range(4)]
                        if k * NCB + cb >= 2 and s3_mode in ('ffn2', 'ffn2nx', 'ffn2nw', 'ffn2nog', 'ffn2dummy', 'full'):
                            t.wait_ge(s_oev.h, (k * NCB + cb - 1) * 4)
                        for piece in range(8):
                            idx = (k * NCB + cb) * 8 + piece
                            t.wait_ge(s_w2l.h[idx % 3], 16 * (idx // 3 + 1))
                            for fcl in range(8):
                                fc = piece * 8 + fcl
                                for ss in range(4):
                                    mm = t.matmul(banks[ss],
                                                  hT[:, fc * SB + ss * 128: fc * SB + (ss + 1) * 128],
                                                  w2b[idx % 3][:, fcl * SB:(fcl + 1) * SB],
                                                  start=(fc == 0), stop=False)
                            mm.then_inc(s_w2d.h, 1)
                            s_w2d.inc()
                        if s3_mode == 'full':
                            if cb == 0:
                                t.wait_ge(s_ul.h, 16 * (k + 1))
                            for piece in range(4):
                                idx = (k * NCB + cb) * 4 + piece
                                t.wait_ge(s_wol.h[idx % 3], 16 * (idx // 3 + 1))
                                for dcl in range(4):
                                    dc = piece * 4 + dcl
                                    for ss in range(4):
                                        mm = t.matmul(banks[ss],
                                                      ub3[:, dc * SB + ss * 128: dc * SB + (ss + 1) * 128],
                                                      wobf[idx % 3][:, dcl * SB:(dcl + 1) * SB],
                                                      start=False, stop=(dc == CC - 1))
                                if piece == 3:
                                    mm.then_inc(s_om.h, 4)
                                    s_om.inc(4)
                                    t.matmul(banks[3][:, 0:128], p_zeros[:],
                                             p_zeros[:], start=False, stop=True,
                                             skip_group_check=True).then_inc(s_wod.h, 1)
                                    s_wod.inc()
                                else:
                                    mm.then_inc(s_wod.h, 1)
                                    s_wod.inc()
                        else:
                            for ss in range(4):
                                mm = t.matmul(banks[ss],
                                              hT[:, ss * 128:(ss + 1) * 128],
                                              w2b[((k * NCB + cb) * 8 + 7) % 3][:, 0:SB],
                                              start=False, stop=True)
                            mm.then_inc(s_om.h, 4)
                            s_om.inc(4)

            @block.scalar
            def _(s):
                s.wait_ge(s_b2.h, b2v)
                for k in range(NSB):
                    for fc in range(FC):
                        s.wait_ge(s_hm.h, k * FC + fc + 1)
                        s.activation(hT[:, fc * SB:(fc + 1) * SB], BK(fc % 2),
                                     AF.Gelu, bias=p_b1T[:, fc:fc + 1]).then_inc(s_gel.h, 1)
                        s_gel.inc()


            @block.vector
            def _(v):
                v.wait_ge(s_bb.h, bb_done)
                for k in range(NSB):
                    if s3_mode in ('ffn1', 'ffn2noev'):
                        continue
                    for cb in range(NCB):
                        for ss in range(4):
                            gidx = (k * NCB + cb) * 4 + ss
                            v.wait_ge(s_om.h, (k * NCB + cb) * 4 + 4)
                            if s3_mode not in ('ffn2nx', 'ffn2dummy'):
                                v.wait_ge(s_xbl.h[ss], 16 * (k * NCB + cb + 1))
                            if gidx >= 2 and s3_mode not in ('ffn2nog', 'ffn2dummy'):
                                v.wait_ge(s_ow.h[gidx % 2], 16 * (gidx // 2))
                            bank = BK((cb % 2) * 4 + ss)
                            if s3_mode == 'ffn2dummy':
                                v.tensor_copy(outb[gidx % 2][:],
                                              bob2[:, cb * SB:(cb + 1) * SB]).then_inc(s_oev.h, 1)
                                s_oev.inc()
                            elif s3_mode != 'ffn2nx':
                                v.tensor_tensor(outb[gidx % 2][:], bank, xbb[ss][:], ALU.add)
                                v.tensor_tensor(outb[gidx % 2][:], outb[gidx % 2][:],
                                                bob2[:, cb * SB:(cb + 1) * SB],
                                                ALU.add).then_inc(s_oev.h, 1)
                            else:
                                v.tensor_tensor(outb[gidx % 2][:], bank[:],
                                                bob2[:, cb * SB:(cb + 1) * SB],
                                                ALU.add).then_inc(s_oev.h, 1)
                            s_oev.inc()

            @block.gpsimd
            def _(g):
                g.wait_ge(s_b2.h, b2v)

                def out_writes(k, cb):
                    if s3_mode in ('ffn1', 'ffn2noev', 'ffn2nog', 'ffn2dummy'):
                        return
                    for ss in range(4):
                        gidx = (k * NCB + cb) * 4 + ss
                        g.wait_ge(s_oev.h, gidx + 1)
                        r0 = k * SB + ss * 128
                        _dst = out_scr if s3_mode == 'ffn2nw' else out_hbm
                        g.dma_start(out=_dst[r0:r0 + 128, cb * SB:(cb + 1) * SB],
                                    in_=outb[gidx % 2][:]).then_inc(s_ow.h[gidx % 2], 16)
                        s_ow.inc(gidx % 2)

                def w2_stream(k, cb):
                    for piece in range(8):
                        idx = (k * NCB + cb) * 8 + piece
                        if idx >= 3:
                            g.wait_ge(s_w2d.h, idx - 2)
                        g.dma_start(
                            out=w2b[idx % 3].ap().rearrange("p (c j) -> p c j", j=SB),
                            in_=w2q[cb, piece * 8:(piece + 1) * 8, :, :].rearrange(
                                "c p j -> p c j")).then_inc(s_w2l.h[idx % 3], 16)
                        s_w2l.inc(idx % 3)

                def u_load_g(k):
                    if k >= 1:
                        g.wait_ge(s_om.h, k * 16)
                    g.dma_start(out=ub3.ap().rearrange("p (h j) -> p h j", j=SB),
                                in_=u_sc[:, k, :, :].rearrange("h p j -> p h j")).then_inc(s_ul.h, 16)
                    s_ul.inc(16)

                if s3_mode == 'full':
                    for i in range(2):
                        g.wait_ge(s_cw.h[i], cache_done[i])
                    seq = [(k, cb) for k in range(NSB) for cb in range(NCB)]
                    u_load_g(0)
                    w2_stream(*seq[0])
                    for j in range(1, len(seq)):
                        if seq[j][1] == 0:
                            u_load_g(seq[j][0])
                        w2_stream(*seq[j])
                        out_writes(*seq[j - 1])
                    out_writes(*seq[-1])
                else:
                    for k in range(NSB):
                        if s3_mode == 'ffn1':
                            continue
                        for cb in range(NCB):
                            out_writes(k, cb)
                if s3_mode == 'ffn1':
                    g.wait_ge(s_gel.h, NSB * FC)
                if s3_mode == 'ffn2noev':
                    g.wait_ge(s_om.h, NSB * 16)
                if s3_mode in ('ffn2nog', 'ffn2dummy'):
                    g.wait_ge(s_oev.h, NSB * NCB * 4)
                if debug:
                    g.dma_start(out=dbg_ht, in_=hT[:]).then_inc(s_owd.h, 16)
                    s_owd.inc(16)
                    g.dma_start(out=dbg_xnt, in_=xnTb[(NSB - 1) % 2][:]).then_inc(s_owd.h, 16)
                    s_owd.inc(16)
                    g.dma_start(out=dbg_ub3, in_=ub3[:]).then_inc(s_owd.h, 16)
                    s_owd.inc(16)
                if debug:
                    g.dma_start(out=dbg_xn, in_=xn_sc).then_inc(s_owd.h, 16)
                    s_owd.inc(16)
                    g.dma_start(out=dbg_u.rearrange("h s p j -> (h s) p j"),
                                in_=u_sc.rearrange("h s p j -> (h s) p j")).then_inc(s_owd.h, 16)
                    s_owd.inc(16)
                    g.dma_start(out=dbg_w1q.rearrange("a b p j -> (a b) p j"),
                                in_=w1q.rearrange("a b p j -> (a b) p j")).then_inc(s_owd.h, 16)
                    s_owd.inc(16)
                    g.dma_start(out=dbg_w2q.rearrange("a b p j -> (a b) p j"),
                                in_=w2q.rearrange("a b p j -> (a b) p j")).then_inc(s_owd.h, 16)
                    s_owd.inc(16)
                g.wait_ge(s_ow.h[0], s_ow.n[0])
                g.wait_ge(s_ow.h[1], s_ow.n[1])
                if s_owd.n:
                    g.wait_ge(s_owd.h, s_owd.n)

    return nc


def kernel(L0_attn_x, L0_attn_Wq, L0_attn_bq, L0_attn_Wk, L0_attn_bk,
           L0_attn_Wv, L0_attn_bv, L0_attn_Wo, L0_attn_bo,
           L0_attn_spiral_weight, L0_attn_helix_rotation,
           L0_ffn_w1, L0_ffn_b1, L0_ffn_w2, L0_ffn_b2):
    f32 = np.float32
    x = np.ascontiguousarray(np.asarray(L0_attn_x, f32))
    wq = np.asarray(L0_attn_Wq, f32); wk = np.asarray(L0_attn_Wk, f32)
    wv = np.asarray(L0_attn_Wv, f32); wo = np.asarray(L0_attn_Wo, f32)
    w1 = np.asarray(L0_ffn_w1, f32); w2 = np.asarray(L0_ffn_w2, f32)

    shared = dict(
        wqT=np.ascontiguousarray(wq.T),
        wkT=np.ascontiguousarray(wk.T),
        wvT=np.ascontiguousarray(wv.T),
        woT=np.ascontiguousarray(wo.T),
        w1s=np.ascontiguousarray(w1.T.reshape(CC, 128, NFS, 512).transpose(2, 0, 1, 3)),
        w2s=np.ascontiguousarray(w2.T.reshape(FC, 128, NCB, 512).transpose(2, 0, 1, 3)),
        bqT=np.ascontiguousarray(np.asarray(L0_attn_bq, f32).reshape(H, 128).T),
        bkT=np.ascontiguousarray(np.asarray(L0_attn_bk, f32).reshape(H, 128).T),
        rotT=np.ascontiguousarray(np.asarray(L0_attn_helix_rotation, f32).reshape(H, 128).T),
        spT=np.ascontiguousarray(np.broadcast_to(np.asarray(L0_attn_spiral_weight, f32), (128, H))),
        b1T=np.ascontiguousarray(np.asarray(L0_ffn_b1, f32).reshape(FC, 128).T),
        bo_rep=np.ascontiguousarray(np.broadcast_to(np.asarray(L0_attn_bo, f32), (128, D))),
        b2_rep=np.ascontiguousarray(np.broadcast_to(np.asarray(L0_ffn_b2, f32), (128, D))),
        bvT=np.ascontiguousarray(np.asarray(L0_attn_bv, f32).reshape(H, 128).T),
    )
    in_maps = [dict(x=np.ascontiguousarray(x[b]), **shared) for b in range(8)]
    nc = build()
    res = run_bass_kernel_spmd(nc, in_maps, core_ids=list(range(8)))
    return np.stack([np.asarray(res.results[b]["out"], f32) for b in range(8)], axis=0)


if __name__ == "__main__":
    build()
    print("built ok")
